# version 18
# baseline (speedup 1.0000x reference)
import numpy as np
import ml_dtypes
from contextlib import ExitStack

import concourse.bass as bass
from concourse import dve_ops
import concourse.mybir as mybir
from concourse.bass_utils import run_bass_kernel_spmd

AF = mybir.ActivationFunctionType
ALU = mybir.AluOpType
AX = mybir.AxisListType
F32 = mybir.dt.float32
BF16 = mybir.dt.bfloat16
BFNP = ml_dtypes.bfloat16

B, T, D = 4, 2048, 4096
H, DH = 32, 128
NT = 16
NOWN = 8
NTOK = NOWN * 128
BOT, RDIM, NSLOT, KTOP, VBOT = 64, 48, 1024, 8, 256
EPS = 1e-5
S_ATT = 1.0 / (DH ** 0.5)
SQRT_DH = float(DH ** 0.5)
GELU_C1 = 0.7978845608028654
GELU_C2 = GELU_C1 * 0.044715


class Builder:
    def __init__(self, debug=False):
        self.debug = debug
        self.nc = bass.Bass("TRN2", target_bir_lowering=False)
        self.es = ExitStack()
        self.pes = None
        self.cnt = {}
        self.sem = {}
        self.memo = {}
        self.dt = {}
        self.dbg = {}

    def mksem(self, *names):
        for n in names:
            self.sem[n] = self.es.enter_context(self.nc.semaphore(n))
            self.cnt[n] = 0

    def inc(self, inst, name, n=1):
        self.cnt[name] += n
        inst.then_inc(self.sem[name], n)
        return (name, self.cnt[name])

    def tok(self, name):
        return (name, self.cnt[name])

    def wait(self, eng, engname, *toks):
        for tk in toks:
            if tk is None:
                continue
            name, v = tk
            if v <= 0:
                continue
            key = (engname, name)
            if self.memo.get(key, 0) >= v:
                continue
            eng.wait_ge(self.sem[name], v)
            self.memo[key] = v

    def pe(self, fn):
        self.block.tensor(lambda t: fn(t))

    def dve(self, fn):
        self.block.vector(lambda v: fn(v))

    def act(self, fn):
        self.block.scalar(lambda a: fn(a))

    def pool(self, fn):
        self.block.gpsimd(lambda g: fn(g))

    def syn(self, fn):
        self.block.sync(lambda s: fn(s))

    def sb(self, name, shape, dtype, side="left"):
        return self.es.enter_context(
            self.nc.sbuf_tensor(name, list(shape), dtype, side=side))

    def sbx(self, name, shape, dtype, side="right"):
        return self.pes.enter_context(
            self.nc.sbuf_tensor(name, list(shape), dtype, side=side))

    def psx(self, name, shape, dtype=F32):
        return self.pes.enter_context(self.nc.psum_tensor(name, list(shape), dtype))

    def scope(self):
        st = ExitStack()
        old = self.pes
        self.pes = st

        class _G:
            def __enter__(sg):
                return sg

            def __exit__(sg, *a):
                st.close()
                self.pes = old
                return False
        return _G()

    def din(self, name, shape, dtype=F32):
        self.dt[name] = self.nc.dram_tensor(name, list(shape), dtype, kind="ExternalInput")
        return self.dt[name]

    def ebcast(self, p_ex, qn, ntiles, own, dest, dest_row=None):
        sc = self
        for j in range(ntiles):
            selap = self.selown_f[:, j:j + 1] if own else self.sel16_f[:, j:j + 1]

            def _t(t, selap=selap, qn=qn):
                sc.wait(t, "PE", self.strows_tok, self.cst_tok, sc.tok("sv"))
                sc.inc(t.matmul(p_ex[0:1, 0:128], selap, self.stat_rows[qn][:, :],
                                start=True, stop=True), "sp", 1)
            self.pe(_t)
            etok = sc.tok("sp")

            def _v(v, j=j, etok=etok):
                sc.wait(v, "DVE", etok)
                sc.inc(v.tensor_copy(self.row_scr[:, :], p_ex[0:1, 0:128]), "sv", 1)
                if dest_row is not None:
                    sc.inc(v.tensor_copy(dest_row[0:1, 128 * j:128 * (j + 1)],
                                         p_ex[0:1, 0:128]), "sv", 1)
            self.dve(_v)
            rtok = sc.tok("sv")
            if dest is not None:
                def _t2(t, rtok=rtok):
                    sc.wait(t, "PE", rtok)
                    sc.inc(t.matmul(p_ex[:, 128:256], self.ones_row_f[:, :],
                                    self.row_scr[:, :], start=True, stop=True),
                           "sp", 1)
                self.pe(_t2)
                btok = sc.tok("sp")

                def _v2(v, j=j, btok=btok):
                    sc.wait(v, "DVE", btok)
                    sc.inc(v.tensor_copy(dest[:, 128 * j:128 * (j + 1)],
                                         p_ex[:, 128:256]), "sv", 1)
                self.dve(_v2)

    def feed_init(self):
        sc = self
        self.f_xkv = [self.sbx("f_xkv0", (128, 32, 128), BF16),
                      self.sbx("f_xkv1", (128, 32, 128), BF16)]
        self.f_gd = self.sbx("f_gd", (128, 128), F32)
        self.f_cols = self.sbx("f_cols", (128, 6), F32)
        self.p_st = self.psx("p_st", [128, 512])
        self.f_n = 48
        self.f_loaded = 0
        self.f_done = 0
        self.f_cons = [None] * self.f_n
        self.f_pst_war = None
        self.f_pst_war_new = None

        def loadk(k):
            b4, t_i = divmod(k + NT, NT)
            p = k % 2

            def _g(gq, k=k, b4=b4, t_i=t_i, p=p):
                if k >= 2:
                    sc.wait(gq, "POOL", self.f_cons[k - 2])
                sc.inc(gq.dma_start(out=self.f_xkv[p].ap(),
                                    in_=self.dt["x_all_t"].ap()[b4, t_i]),
                       "d_kv%d" % p, 16)
            self.pool(_g)
            self.f_loaded += 1
        self._feed_loadk = loadk
        loadk(0)
        loadk(1)

    def feed_step(self):
        sc = self
        if self.f_done >= self.f_n:
            return
        k = self.f_done
        self.f_done += 1
        p = k % 2
        ld = sc.tok("d_kv%d" % p)
        FS1, FS2, FM, FVAR, FLNV, FTMP = range(6)

        def fcol(q):
            return self.f_cols[:, q:q + 1]

        def _t(t, k=k, p=p, ld=ld, pst_war=self.f_pst_war):
            sc.wait(t, "PE", ld, self.cst_tok, pst_war)
            self.mmacc(t, self.p_st[:, 0:128],
                       [(self.f_xkv[p][:, c, :], self.f_xkv[p][:, c, :])
                        for c in range(32)])
        self.pe(_t)
        gm_tok = sc.tok("sp")

        def _v(v, gm_tok=gm_tok):
            sc.wait(v, "DVE", gm_tok)
            sc.inc(v.tensor_tensor(self.f_gd[:, :], self.p_st[:, 0:128],
                                   self.eye_f[:, :], ALU.mult), "sv", 1)
            sc.wait(v, "DVE", sc.tok("sv"))
            sc.inc(v.reduce_sum(fcol(FS2), self.f_gd[:, :], axis=AX.X), "sv", 1)
        self.dve(_v)
        gd_tok = sc.tok("sv")

        def _t2(t, k=k, p=p, gd_tok=gd_tok):
            sc.wait(t, "PE", gd_tok)
            self.mmacc(t, self.p_st[:, 128:129],
                       [(self.f_xkv[p][:, c, :], self.ones_col_b[:, :])
                        for c in range(32)])
            self.f_cons[k] = sc.tok("sp")
        self.pe(_t2)
        cs_tok = sc.tok("sp")

        def _v2(v, cs_tok=cs_tok):
            sc.wait(v, "DVE", cs_tok)
            sc.inc(v.tensor_scalar(fcol(FM), self.p_st[:, 128:129], 1.0 / D, None,
                                   ALU.mult), "sv", 1)
            self.f_pst_war_new = sc.tok("sv")
            sc.inc(v.tensor_scalar(fcol(FVAR), fcol(FS2), 1.0 / D, None, ALU.mult),
                   "sv", 1)
            sc.wait(v, "DVE", sc.tok("sv"))
            sc.inc(v.tensor_tensor(fcol(FTMP), fcol(FM), fcol(FM), ALU.mult), "sv", 1)
            sc.wait(v, "DVE", sc.tok("sv"))
            sc.inc(v.tensor_tensor(fcol(FVAR), fcol(FVAR), fcol(FTMP), ALU.subtract),
                   "sv", 1)
        self.dve(_v2)
        var_tok = sc.tok("sv")

        def _a(a, var_tok=var_tok):
            sc.wait(a, "ACT", self.cb_tok, var_tok)
            sc.inc(a.activation(fcol(FLNV), fcol(FVAR), AF.Ln, bias=EPS), "sa", 1)
            sc.wait(a, "ACT", sc.tok("sa"))
            sc.inc(a.activation(fcol(FTMP), fcol(FLNV), AF.Exp, scale=-1.0), "sa", 1)
        self.act(_a)
        u2_tok = sc.tok("sa")

        def _v3(v, u2_tok=u2_tok):
            sc.wait(v, "DVE", u2_tok)
            sc.inc(v.tensor_tensor(fcol(FTMP), fcol(FVAR), fcol(FTMP), ALU.mult),
                   "sv", 1)
        self.dve(_v3)
        vh_tok = sc.tok("sv")

        def _a2(a, vh_tok=vh_tok):
            sc.wait(a, "ACT", vh_tok)
            sc.inc(a.activation(fcol(FTMP), fcol(FTMP), AF.Ln, bias=1.0), "sa", 1)
        self.act(_a2)
        lg_tok = sc.tok("sa")

        def _v4(v, lg_tok=lg_tok):
            sc.wait(v, "DVE", lg_tok)
            sc.inc(v.tensor_tensor(self.lacc[:, :], self.lacc[:, :], fcol(FTMP),
                                   ALU.add), "sv", 1)
        self.dve(_v4)
        self.f_pst_war = self.f_pst_war_new
        if self.f_loaded < self.f_n:
            self._feed_loadk(self.f_loaded)

    def feed_drain(self):
        while self.f_done < self.f_n:
            self.feed_step()

    def csum_and_gate(self, p_any):
        sc = self

        def _t6(t):
            sc.wait(t, "PE", sc.tok("sv"), sc.tok("sa"))
            sc.inc(t.transpose(p_any[0:1, 0:128], self.lacc[:, :], self.eye_f[:, :]),
                   "sp", 1)
        self.pe(_t6)
        la_tok = sc.tok("sp")

        def _v8(v, la_tok=la_tok):
            sc.wait(v, "DVE", la_tok)
            sc.inc(v.tensor_copy(self.f_gd[0:1, 0:128], p_any[0:1, 0:128]), "sv", 1)
            sc.wait(v, "DVE", sc.tok("sv"))
            sc.inc(v.reduce_sum(self.csum_sb[0:1, 0:1], self.f_gd[0:1, 0:128],
                                axis=AX.X), "sv", 1)
        self.dve(_v8)
        self.csum_tok = sc.tok("sv")
        if self.debug:
            def _sdbg(sq):
                sc.wait(sq, "POOL", self.csum_tok)
                sc.inc(sq.dma_start(out=self.dbg["dbg_csum"].ap()[0:1, 0:1],
                                    in_=self.csum_sb.ap()[0:1, 0:1]), "d_dbg", 16)
                sq.wait_ge(self.sem["d_dbg"], self.cnt["d_dbg"])
            self.pool(_sdbg)
        self.gate_math()

    def gate_math(self):
        sc = self
        self.row2 = self.sbx("row2", (1, NTOK), F32)
        self.row3 = self.sbx("row3", (1, NTOK), F32)

        def _v(v):
            sc.wait(v, "DVE", self.csum_tok)
            sc.inc(v.tensor_scalar(self.row2[0:1, 0:1], self.csum_sb[0:1, 0:1],
                                   1.0 / 8192.0, 1e-6, ALU.mult, ALU.add), "sv", 1)
            sc.wait(v, "DVE", sc.tok("sv"))
            sc.inc(v.reciprocal(self.row2[0:1, 1:2], self.row2[0:1, 0:1]), "sv", 1)
            sc.inc(v.tensor_scalar(self.row2[0:1, 2:3], self.scal_sb[:, 2:3], -1.0,
                                   None, ALU.mult), "sv", 1)
        self.dve(_v)
        r2_tok = sc.tok("sv")

        def _a(a, r2_tok=r2_tok):
            sc.wait(a, "ACT", r2_tok)
            sc.inc(a.activation(self.row3[0:1, :], self.learned_row[0:1, :],
                                AF.Exp, scale=-1.0, bias=self.row2[0:1, 2:3]),
                   "sa", 1)
        self.act(_a)
        sig_tok = sc.tok("sa")

        def _v2a(v, sig_tok=sig_tok):
            sc.wait(v, "DVE", sig_tok)
            sc.inc(v.tensor_scalar(self.row3[:, :], self.row3[:, :], 1.0, None,
                                   ALU.add), "sv", 1)
        self.dve(_v2a)
        d1_tok = sc.tok("sv")

        def _a1r(a, d1_tok=d1_tok):
            sc.wait(a, "ACT", d1_tok)
            sc.inc(a.activation(self.row3[:, :], self.row3[:, :], AF.Ln), "sa", 1)
            sc.wait(a, "ACT", sc.tok("sa"))
            sc.inc(a.activation(self.row3[:, :], self.row3[:, :], AF.Exp,
                                scale=-1.0), "sa", 1)
        self.act(_a1r)
        s1_tok = sc.tok("sa")

        def _v2(v, s1_tok=s1_tok):
            sc.wait(v, "DVE", s1_tok)
            sc.inc(v.tensor_scalar(self.row3[:, :], self.row3[:, :], 2.5, None,
                                   ALU.mult), "sv", 1)
            sc.inc(v.tensor_scalar(self.gate_row[:, :], self.logv_row[:, :],
                                   self.row2[0:1, 1:2], 0.5, ALU.mult, ALU.mult),
                   "sv", 1)
            sc.wait(v, "DVE", sc.tok("sv"))
            sc.inc(v.tensor_tensor(self.gate_row[:, :], self.gate_row[:, :],
                                   self.row3[:, :], ALU.add), "sv", 1)
            sc.wait(v, "DVE", sc.tok("sv"))
            sc.inc(v.tensor_scalar(self.gate_row[:, :], self.gate_row[:, :], 0.0, 5.0,
                                   ALU.max, ALU.min), "sv", 1)
            sc.wait(v, "DVE", sc.tok("sv"))
            sc.inc(v.tensor_scalar(self.gate_row[:, :], self.gate_row[:, :],
                                   1.0 / 1.5, -0.5 / 1.5, ALU.mult, ALU.add), "sv", 1)
            sc.wait(v, "DVE", sc.tok("sv"))
            sc.inc(v.tensor_scalar(self.gate_row[:, :], self.gate_row[:, :], 0.0, 1.0,
                                   ALU.max, ALU.min), "sv", 1)
            sc.wait(v, "DVE", sc.tok("sv"))
            sc.inc(v.tensor_scalar(self.gate_row[:, :], self.gate_row[:, :],
                                   self.scal_sb[:, 0:1], self.scal_sb[:, 1:2],
                                   ALU.mult, ALU.add), "sv", 1)
        self.dve(_v2)
        pre_tok = sc.tok("sv")

        def _a2(a, pre_tok=pre_tok):
            sc.wait(a, "ACT", pre_tok)
            sc.inc(a.activation(self.row3[:, :], self.gate_row[:, :], AF.Exp,
                                scale=-1.0), "sa", 1)
        self.act(_a2)
        e2_tok = sc.tok("sa")

        def _v3a(v, e2_tok=e2_tok):
            sc.wait(v, "DVE", e2_tok)
            sc.inc(v.tensor_scalar(self.row3[:, :], self.row3[:, :], 1.0, None,
                                   ALU.add), "sv", 1)
        self.dve(_v3a)
        d2_tok = sc.tok("sv")

        def _a2r(a, d2_tok=d2_tok):
            sc.wait(a, "ACT", d2_tok)
            sc.inc(a.activation(self.row3[:, :], self.row3[:, :], AF.Ln), "sa", 1)
            sc.wait(a, "ACT", sc.tok("sa"))
            sc.inc(a.activation(self.gate_row[:, :], self.row3[:, :], AF.Exp,
                                scale=-1.0), "sa", 1)
        self.act(_a2r)
        g0_tok = sc.tok("sa")

        def _v3(v, g0_tok=g0_tok):
            sc.wait(v, "DVE", g0_tok)
            sc.inc(v.tensor_scalar(self.row3[:, :], self.gate_row[:, :], 0.05, None,
                                   ALU.is_ge), "sv", 1)
            sc.wait(v, "DVE", sc.tok("sv"))
            sc.inc(v.tensor_tensor(self.gate_row[:, :], self.gate_row[:, :],
                                   self.row3[:, :], ALU.mult), "sv", 1)
        self.dve(_v3)
        self.gate_tok = sc.tok("sv")
        if self.debug:
            def _sdbg(sq):
                sc.wait(sq, "POOL", self.gate_tok)
                sc.inc(sq.dma_start(out=self.dbg["dbg_gate"].ap(),
                                    in_=self.gate_row.ap()), "d_dbg", 16)
                sq.wait_ge(self.sem["d_dbg"], self.cnt["d_dbg"])
            self.pool(_sdbg)

    def bar(self):
        toks = [self.tok(n) for n in ("sp", "sv", "sa", "sg")]
        for emit, name in [(self.pe, "PE"), (self.dve, "DVE"), (self.act, "ACT"),
                           (self.pool, "POOL"), (self.syn, "SYNC")]:
            def _f(e, name=name):
                self.wait(e, name, *toks)
            emit(_f)

    def mmacc(self, t, out_ap, ops, inc=True):
        n = len(ops)
        m = None
        for i, (l, r) in enumerate(ops):
            m = t.matmul(out_ap, l, r, start=(i == 0), stop=(i == n - 1))
        if inc:
            return self.inc(m, "sp", 1)
        return None

    def build(self):
        nc = self.nc
        din = self.din
        din("x_all_t", (4, NT, 128, 32, 128))
        din("xm_t", (128, 32, NTOK))
        din("xmT", (D, NTOK))
        din("wq_t", (32, 128, 32, 128))
        din("wo_t", (32, 128, 32, 128))
        din("wk_t", (128, 32, 128))
        din("wv_t", (128, 32, 128))
        din("wqf_t", (128, 32, BOT))
        din("wr_T", (BOT, RDIM))
        din("akT", (BOT, NSLOT))
        din("avT", (D, NSLOT))
        din("wvdT", (D, VBOT))
        din("wvuT_t", (128, 2, D))
        din("wu1T_t", (8, 128, 32, 128))
        din("wu2c", (128, 8))
        din("bu1c", (128, 8))
        din("logrel_r", (1, NSLOT))
        din("scal", (1, 4))
        din("masks", (128, 8, 2, 128), BF16)
        din("c_ones_row_b", (1, 128), BF16)
        din("c_ones_col_b", (128, 1), BF16)
        din("c_ones_row_f", (1, 128), F32)
        din("c_ones_mat_b", (128, 128), BF16)
        din("c_eye_b", (128, 128), BF16)
        din("c_eye_f", (128, 128), F32)
        din("c_sel16_f", (16, 16), F32)
        din("c_selown_f", (16, 8), F32)

        self.outT = nc.dram_tensor("outT", [D, NTOK], F32, kind="ExternalOutput")
        if self.debug:
            for name, shape in [
                ("dbg_stat", [128, 10, 16]), ("dbg_K", [128, T]),
                ("dbg_V", [128, NT, 128]), ("dbg_q", [128, 4, NTOK]),
                ("dbg_query", [BOT, NTOK]), ("dbg_rs", [128, NSLOT]),
                ("dbg_w", [128, NSLOT]), ("dbg_outv", [128, 2, NTOK]),
                ("dbg_gate", [1, NTOK]), ("dbg_att", [128, 4, NTOK]),
                ("dbg_csum", [1, 8]), ("dbg_negms", [32, NTOK]),
            ]:
                self.dbg[name] = nc.dram_tensor(name, shape, F32, kind="ExternalOutput")

        self.mksem("sp", "sv", "sa", "sg",
                   "d_cst", "d_cstg", "d_xb0", "d_xb1", "d_kv0", "d_kv1", "d_xm",
                   "d_av0", "d_av1", "d_wvd", "d_wu0",
                   "d_wq0", "d_wq1", "d_wq2", "d_wq3",
                   "d_wq4", "d_wq5", "d_wq6", "d_wq7",
                   "d_wo0", "d_wo1", "d_xf0", "d_xf1",
                   "d_out", "d_o0", "d_o1", "d_dbg")

        sb = self.sb
        self.ones_row_b = sb("ones_row_b", (1, 128), BF16)
        self.ones_col_b = sb("ones_col_b", (128, 1), BF16)
        self.ones_row_f = sb("ones_row_f", (1, 128), F32)
        self.ones_mat_b = sb("ones_mat_b", (128, 128), BF16)
        self.eye_b = sb("eye_b", (128, 128), BF16)
        self.eye_f = sb("eye_f", (128, 128), F32)
        self.sel16_f = sb("sel16_f", (16, 16), F32)
        self.selown_f = sb("selown_f", (16, 8), F32)
        self.cb = sb("cb", (128, 2), F32)
        self.K_bf = sb("K_bf", (128, T), BF16)
        self.V_bf = sb("V_bf", (128, NT, 128), BF16)
        self.u_own_b = sb("u_own_b", (128, NTOK), F32)
        self.outv_bf = sb("outv_bf", (128, 2, NTOK), BF16)
        self.wu2c_b = sb("wu2c_b", (128, 8), BF16)
        self.bu1c_f = sb("bu1c_f", (128, 8), F32)
        self.scal_sb = sb("scal_sb", (1, 4), F32)
        self.row_scr = sb("row_scr", (1, 128), F32)
        self.gate_row = sb("gate_row", (1, NTOK), F32)
        self.statc = sb("statc", (128, 10, 16), F32)
        self.lacc = sb("lacc", (128, 1), F32)
        self.learned_row = sb("learned_row", (1, NTOK), F32)
        self.logv_row = sb("logv_row", (1, NTOK), F32)
        self.stat_rows = {}
        for qn in ["m", "u", "u1", "mu", "logv"]:
            self.stat_rows[qn] = sb("str_%s" % qn, (16, 128), F32)
        self.big_a = sb("big_a", (128, 32, NTOK), BF16, side="right")

        nc.const_aps.register(F32, EPS, self.cb[:, 0:1])
        nc.const_aps.register(F32, 1.0, self.cb[:, 1:2])

        self.block = self.es.enter_context(nc.Block())

        with self.scope():
            self.p0()
        with self.scope():
            self.p_a()
        self.bar()
        with self.scope():
            self.p_own()
        self.bar()
        with self.scope():
            self.q_bf = self.sbx("q_bf", (128, 32, NTOK), BF16)
            self.feed_init()
            with self.scope():
                self.p_b()
            self.bar()
            with self.scope():
                self.p_c()
            self.feed_drain()
            self.csum_and_gate(self.p_st)
        self.bar()
        with self.scope():
            self.p_d()
        self.finish()
        self.es.close()
        return nc

    def p0(self):
        sc = self

        def _s(sq):
            for name, dst in [("c_ones_row_b", self.ones_row_b),
                              ("c_ones_col_b", self.ones_col_b),
                              ("c_ones_row_f", self.ones_row_f),
                              ("c_ones_mat_b", self.ones_mat_b),
                              ("c_eye_b", self.eye_b), ("c_eye_f", self.eye_f),
                              ("c_sel16_f", self.sel16_f), ("c_selown_f", self.selown_f),
                              ("scal", self.scal_sb), ("bu1c", self.bu1c_f)]:
                sc.inc(sq.dma_start(out=dst.ap(), in_=self.dt[name].ap()), "d_cst", 16)
        self.syn(_s)
        self.cst_tok = sc.tok("d_cst")

        def _g(gq):
            sc.inc(gq.dma_start(out=self.wu2c_b.ap(), in_=self.dt["wu2c"].ap()),
                   "d_cstg", 16)
        self.pool(_g)
        self.cstg_tok = sc.tok("d_cstg")

        def _v(v):
            v.memset(self.cb[:, 0:1], EPS)
            sc.inc(v.memset(self.cb[:, 1:2], 1.0), "sv", 1)
        self.dve(_v)
        self.cb_tok = sc.tok("sv")

        def _a(a):
            from concourse.hw_specs import get_activation_tables
            for idx, tname in enumerate(get_activation_tables(self.nc.m.arch).keys()):
                if tname == "natural_log_exp_and_others":
                    a.add_instruction(mybir.InstLoadActFuncSet(
                        name=self.nc.get_next_instruction_name(),
                        act_func_set_id=idx, ins=[], outs=[]))
                    break
        self.act(_a)


    def p_a(self):
        sc = self
        S1, S2, M_, VAR, U_, LNV, U1C, MUC, LOGV, TMPC = range(10)
        self.U_IDX = U_
        statc = self.statc

        def col(q, t):
            return statc[:, q, t:t + 1]
        self.col = col

        xkv = [self.sbx("xkv0", (128, 32, 128), BF16),
               self.sbx("xkv1", (128, 32, 128), BF16)]
        gd_scr = self.sbx("gd_scr", (128, 128), F32)
        mb_scr = self.sbx("mb_scr", (128, 128), F32)
        ub_scr = self.sbx("ub_scr", (128, 128), F32)
        mrow_scr = self.sbx("mrow_scr", (1, 128), F32)
        urow_scr = self.sbx("urow_scr", (1, 128), F32)
        lacc = self.lacc
        self.csum_sb = self.sb("csum_sb", (1, 4), F32)
        self.wkv_sb = self.sbx("wkv_sb", (128, 2, 32, 128), BF16)
        p_G = self.psx("p_G", [128, 512])
        p_Sm = self.psx("p_Sm", [128, 512])
        p_tr = self.psx("p_tr", [128, 512])
        p_MbU = self.psx("p_MbU", [128, 512])
        p_k = [self.psx("p_k0", [128, 512]), self.psx("p_k1", [128, 512])]
        p_v = [self.psx("p_v0", [128, 512]), self.psx("p_v1", [128, 512])]

        def _gw(gq):
            sc.inc(gq.dma_start(out=self.wkv_sb[:, 0], in_=self.dt["wk_t"].ap()),
                   "d_cstg", 16)
            sc.inc(gq.dma_start(out=self.wkv_sb[:, 1], in_=self.dt["wv_t"].ap()),
                   "d_cstg", 16)
        self.pool(_gw)
        self.cstg_tok = sc.tok("d_cstg")

        def _vz(v):
            sc.inc(v.memset(lacc[:, :], 0.0), "sv", 1)
        self.dve(_vz)

        kv_cons = [None] * NT
        kv_evac = [None] * NT

        def load(i):
            t_i = i
            p = i % 2

            def _g(gq):
                if i >= 2:
                    sc.wait(gq, "POOL", kv_cons[i - 2])
                sc.inc(gq.dma_start(out=xkv[p].ap(),
                                    in_=self.dt["x_all_t"].ap()[0, t_i]),
                       "d_kv%d" % p, 16)
                sc.inc(gq.dma_start(out=self.big_a.ap()[:, 2 * i:2 * i + 2, :],
                                    in_=self.dt["xm_t"].ap()[:, 2 * i:2 * i + 2, :]),
                       "d_xm", 16)
            self.pool(_g)

        load(0)
        load(1)
        pa_pst_war = None
        for i in range(NT):
            t_i = i
            own = True
            p = i % 2
            ld = sc.tok("d_kv%d" % p)

            def _t(t, i=i, p=p, ld=ld, pa_pst_war=pa_pst_war):
                sc.wait(t, "PE", ld, self.cst_tok, pa_pst_war)
                self.mmacc(t, p_G[:, 0:128],
                           [(xkv[p][:, c, :], xkv[p][:, c, :]) for c in range(32)])
                self.mmacc(t, p_Sm[:, 0:1],
                           [(xkv[p][:, c, :], self.ones_col_b[:, :])
                            for c in range(32)])
                if not own:
                    kv_cons[i] = sc.tok("sp")
            self.pe(_t)
            gm_tok = sc.tok("sp")

            def _v(v, i=i, t_i=t_i, gm_tok=gm_tok):
                sc.wait(v, "DVE", gm_tok)
                sc.inc(v.tensor_tensor(gd_scr[:, :], p_G[:, 0:128], self.eye_f[:, :],
                                       ALU.mult), "sv", 1)
                sc.wait(v, "DVE", sc.tok("sv"))
                sc.inc(v.reduce_sum(col(S2, t_i), gd_scr[:, :], axis=AX.X), "sv", 1)
                sc.inc(v.tensor_copy(col(S1, t_i), p_Sm[:, 0:1]), "sv", 1)
                sc.wait(v, "DVE", sc.tok("sv"))
                sc.inc(v.tensor_scalar(col(M_, t_i), col(S1, t_i), 1.0 / D, None,
                                       ALU.mult), "sv", 1)
                sc.inc(v.tensor_scalar(col(VAR, t_i), col(S2, t_i), 1.0 / D, None,
                                       ALU.mult), "sv", 1)
                sc.wait(v, "DVE", sc.tok("sv"))
                sc.inc(v.tensor_tensor(col(TMPC, t_i), col(M_, t_i), col(M_, t_i),
                                       ALU.mult), "sv", 1)
                sc.wait(v, "DVE", sc.tok("sv"))
                sc.inc(v.tensor_tensor(col(VAR, t_i), col(VAR, t_i), col(TMPC, t_i),
                                       ALU.subtract), "sv", 1)
            self.dve(_v)
            var_tok = sc.tok("sv")
            pa_pst_war = var_tok

            def _a(a, t_i=t_i, var_tok=var_tok):
                sc.wait(a, "ACT", self.cb_tok, var_tok)
                sc.inc(a.activation(col(LNV, t_i), col(VAR, t_i), AF.Ln, bias=EPS),
                       "sa", 1)
                sc.wait(a, "ACT", sc.tok("sa"))
                sc.inc(a.activation(col(U_, t_i), col(LNV, t_i), AF.Exp, scale=-0.5),
                       "sa", 1)
            self.act(_a)
            u_tok = sc.tok("sa")

            def _v2(v, t_i=t_i, own=own, u_tok=u_tok):
                sc.wait(v, "DVE", u_tok)
                if own:
                    sc.inc(v.tensor_scalar(col(U1C, t_i), col(U_, t_i), 1.0, None,
                                           ALU.add), "sv", 1)
                    sc.inc(v.tensor_tensor(col(MUC, t_i), col(M_, t_i), col(U_, t_i),
                                           ALU.mult), "sv", 1)
                sc.inc(v.tensor_tensor(col(TMPC, t_i), col(VAR, t_i), col(U_, t_i),
                                       ALU.mult), "sv", 1)
                sc.wait(v, "DVE", sc.tok("sv"))
                sc.inc(v.tensor_tensor(col(TMPC, t_i), col(TMPC, t_i), col(U_, t_i),
                                       ALU.mult), "sv", 1)
            self.dve(_v2)
            vh_tok = sc.tok("sv")

            def _a2(a, t_i=t_i, own=own, vh_tok=vh_tok):
                sc.wait(a, "ACT", vh_tok)
                sc.inc(a.activation(col(LOGV, t_i) if own else col(TMPC, t_i),
                                    col(TMPC, t_i), AF.Ln, bias=1.0), "sa", 1)
            self.act(_a2)
            lg_tok = sc.tok("sa")

            def _v3(v, t_i=t_i, own=own, lg_tok=lg_tok):
                sc.wait(v, "DVE", lg_tok)
                sc.inc(v.tensor_tensor(lacc[:, :], lacc[:, :],
                                       col(LOGV, t_i) if own else col(TMPC, t_i),
                                       ALU.add), "sv", 1)
            self.dve(_v3)

            if own:
                mrow_war = sc.tok("sp")

                def _t2(t, t_i=t_i, vh_tok=vh_tok):
                    sc.wait(t, "PE", vh_tok, sc.tok("sv"))
                    sc.inc(t.transpose(p_tr[0:1, 0:128],
                                       col(M_, t_i), self.eye_f[:, :]), "sp", 1)
                self.pe(_t2)
                trm_tok = sc.tok("sp")

                def _v4(v, trm_tok=trm_tok):
                    sc.wait(v, "DVE", trm_tok)
                    sc.inc(v.tensor_copy(mrow_scr[:, :], p_tr[0:1, 0:128]), "sv", 1)
                self.dve(_v4)
                mr_tok = sc.tok("sv")

                def _t3(t, t_i=t_i, mr_tok=mr_tok):
                    sc.wait(t, "PE", mr_tok)
                    sc.inc(t.matmul(p_MbU[:, 0:128], self.ones_row_f[:, :],
                                    mrow_scr[:, :], start=True, stop=True), "sp", 1)
                    sc.inc(t.transpose(p_tr[0:1, 128:256],
                                       col(U_, t_i), self.eye_f[:, :]), "sp", 1)
                self.pe(_t3)
                mbu_tok = sc.tok("sp")

                def _v5(v, p=p, mbu_tok=mbu_tok):
                    sc.wait(v, "DVE", mbu_tok)
                    sc.inc(v.tensor_copy(mb_scr[:, :], p_MbU[:, 0:128]), "sv", 1)
                    sc.inc(v.tensor_copy(urow_scr[:, :], p_tr[0:1, 128:256]), "sv", 1)
                self.dve(_v5)
                mb_tok = sc.tok("sv")

                def _t4(t, mb_tok=mb_tok):
                    sc.wait(t, "PE", mb_tok)
                    sc.inc(t.matmul(p_MbU[:, 128:256], self.ones_row_f[:, :],
                                    urow_scr[:, :], start=True, stop=True), "sp", 1)
                self.pe(_t4)
                ub_tok = sc.tok("sp")

                def _v6(v, p=p, ub_tok=ub_tok, mb_tok=mb_tok):
                    sc.wait(v, "DVE", ub_tok)
                    sc.inc(v.tensor_copy(ub_scr[:, :], p_MbU[:, 128:256]), "sv", 1)
                    m_sl = mb_scr[:, :].unsqueeze(1).broadcast_to((128, 32, 128))
                    sc.wait(v, "DVE", sc.tok("sv"))
                    sc.inc(v.tensor_tensor(xkv[p][:, :, :], xkv[p][:, :, :], m_sl,
                                           ALU.subtract), "sv", 1)
                self.dve(_v6)
                xc_tok = sc.tok("sv")

                def _t5(t, p=p, i=i, xc_tok=xc_tok):
                    sc.wait(t, "PE", self.cstg_tok, xc_tok)
                    if i >= 2:
                        sc.wait(t, "PE", kv_evac[i - 2])
                    self.mmacc(t, p_k[p][:, 0:128],
                               [(self.wkv_sb[:, 0, c, :], xkv[p][:, c, :])
                                for c in range(32)])
                    self.mmacc(t, p_v[p][:, 0:128],
                               [(xkv[p][:, c, :], self.wkv_sb[:, 1, c, :])
                                for c in range(32)])
                    kv_cons[i] = sc.tok("sp")
                self.pe(_t5)
                mm_tok = sc.tok("sp")

                def _v7(v, t_i=t_i, p=p, i=i, mm_tok=mm_tok):
                    sc.wait(v, "DVE", mm_tok)
                    sc.inc(v.tensor_tensor(self.K_bf[:, 128 * t_i:128 * (t_i + 1)],
                                           p_k[p][:, 0:128], ub_scr[:, :],
                                           ALU.mult), "sv", 1)
                    sc.inc(v.tensor_scalar(self.V_bf[:, t_i, :], p_v[p][:, 0:128],
                                           col(U_, t_i), None, ALU.mult), "sv", 1)
                    kv_evac[i] = sc.tok("sv")
                self.dve(_v7)

            if i == NT - 1:
                self._emit_stat_rows(p_tr)

            if i + 2 < NT:
                load(i + 2)

        self.kv_done = sc.tok("sv")
        if self.debug:
            def _sdbg(sq):
                sc.wait(sq, "POOL", self.kv_done, sc.tok("sa"))
                sc.inc(sq.dma_start(out=self.dbg["dbg_K"].ap(), in_=self.K_bf.ap()),
                       "d_dbg", 16)
                sc.inc(sq.dma_start(out=self.dbg["dbg_V"].ap(), in_=self.V_bf.ap()),
                       "d_dbg", 16)
                sc.inc(sq.dma_start(out=self.dbg["dbg_stat"].ap(), in_=statc.ap()),
                       "d_dbg", 16)
                sq.wait_ge(self.sem["d_dbg"], self.cnt["d_dbg"])
            self.pool(_sdbg)

    def _emit_stat_rows(self, p_tr):
        sc = self
        S1, S2, M_, VAR, U_, LNV, U1C, MUC, LOGV, TMPC = range(10)
        stat_v = sc.tok("sv")
        stat_a = sc.tok("sa")
        for qi, qn in [(M_, "m"), (U_, "u"), (U1C, "u1"), (MUC, "mu"),
                       (LOGV, "logv")]:
            def _t(t, qi=qi):
                sc.wait(t, "PE", self.cst_tok, stat_v, stat_a, sc.tok("sv"))
                sc.inc(t.transpose(p_tr[0:16, 128:256], self.statc[:, qi, :],
                                   self.eye_f[:, :]), "sp", 1)
            self.pe(_t)
            ttok = sc.tok("sp")

            def _v(v, qn=qn, ttok=ttok):
                sc.wait(v, "DVE", ttok)
                sc.inc(v.tensor_copy(self.stat_rows[qn][:, :], p_tr[0:16, 128:256]),
                       "sv", 1)
            self.dve(_v)
        self.strows_tok = sc.tok("sv")
        self.ebcast(p_tr, "u", NOWN, True, self.u_own_b)
        self.ebcast(p_tr, "logv", NOWN, True, None, dest_row=self.logv_row)
        self.bcast_tok = sc.tok("sv")

    def p_own(self):
        sc = self
        big = self.big_a
        self.akT_bf = self.sbx("akT_bf", (BOT, NSLOT), BF16)
        self.rkT_bf = self.sbx("rkT_bf", (RDIM, NSLOT), BF16)
        self.wrT_bf = self.sbx("wrT_bf", (BOT, RDIM), BF16)
        self.logrel_bf = self.sbx("logrel_bf", (1, NSLOT), BF16)
        self.query_bf = self.sbx("query_bf", (BOT, NTOK), BF16)
        self.wqf_sb = self.sbx("wqf_sb", (128, 32, BOT), BF16)
        self.m_own_b = self.sbx("m_own_b", (128, NTOK), F32)
        self.rq_bf = self.sbx("rq_bf", (RDIM, NTOK), BF16)

        def _gl(gq):
            for name, dst in [("akT", self.akT_bf), ("logrel_r", self.logrel_bf),
                              ("wr_T", self.wrT_bf), ("wqf_t", self.wqf_sb)]:
                sc.inc(gq.dma_start(out=dst.ap(), in_=self.dt[name].ap()),
                       "d_cstg", 16)
        self.pool(_gl)
        self.cstg_tok = sc.tok("d_cstg")

        with self.scope():
            p_exo = self.psx("p_exo", [128, 512])
            self.ebcast(p_exo, "m", NOWN, True, self.m_own_b)
            mo_tok = sc.tok("sv")

            for hh in range(2):
                def _trk(t, hh=hh):
                    sc.wait(t, "PE", self.cstg_tok, sc.tok("sv"))
                    sc.inc(t.matmul(p_exo[0:RDIM, 0:512], self.wrT_bf[:, :],
                                    self.akT_bf[:, 512 * hh:512 * (hh + 1)],
                                    start=True, stop=True), "sp", 1)
                self.pe(_trk)
                ptok = sc.tok("sp")

                def _vrk(v, hh=hh, ptok=ptok):
                    sc.wait(v, "DVE", ptok)
                    sc.inc(v.tensor_copy(self.rkT_bf[:, 512 * hh:512 * (hh + 1)],
                                         p_exo[0:RDIM, 0:512]), "sv", 1)
                self.dve(_vrk)

        xm_ld = sc.tok("d_xm")

        def _vsub(v):
            sc.wait(v, "DVE", xm_ld, mo_tok)
            for c in range(32):
                sc.inc(v.tensor_tensor(big[:, c, :], big[:, c, :], self.m_own_b[:, :],
                                       ALU.subtract), "sv", 1)
        self.dve(_vsub)
        self.xm_tok = sc.tok("sv")

        with self.scope():
            p_q = [self.psx("p_q0", [BOT, 512]), self.psx("p_q1", [BOT, 512])]
            for th in range(2):
                def _t(t, th=th):
                    sc.wait(t, "PE", self.cstg_tok, self.xm_tok)
                    self.mmacc(t, p_q[th][:, :],
                               [(self.wqf_sb[:, c, :], big[:, c, 512 * th:512 * (th + 1)])
                                for c in range(32)])
                self.pe(_t)
                qtok = sc.tok("sp")

                def _v(v, th=th, qtok=qtok):
                    sc.wait(v, "DVE", qtok)
                    sc.inc(v.scalar_tensor_tensor(
                        self.query_bf[:, 512 * th:512 * (th + 1)], p_q[th][:, :], 0.125,
                        self.u_own_b[0:BOT, 512 * th:512 * (th + 1)], ALU.mult,
                        ALU.mult), "sv", 1)
                self.dve(_v)
            query_tok = sc.tok("sv")

            pass
            p_rq = self.psx("p_rq", [RDIM, 512])
            for th in range(2):
                def _t(t, th=th, query_tok=query_tok):
                    sc.wait(t, "PE", query_tok, sc.tok("sv"))
                    sc.inc(t.matmul(p_rq[:, :], self.wrT_bf[:, :],
                                    self.query_bf[:, 512 * th:512 * (th + 1)],
                                    start=True, stop=True), "sp", 1)
                self.pe(_t)
                rqt = sc.tok("sp")

                def _v(v, th=th, rqt=rqt):
                    sc.wait(v, "DVE", rqt)
                    sc.inc(v.tensor_scalar(self.rq_bf[:, 512 * th:512 * (th + 1)],
                                           p_rq[:, :], 8.0 / float(48.0 ** 0.5), None,
                                           ALU.mult), "sv", 1)
                self.dve(_v)
            self.rq_tok = sc.tok("sv")
        query_tok = sc.tok("sv")

        self.bar()
        self.AVd_bf = self.sbx("AVd_bf", (128, 8, VBOT), BF16)
        with self.scope():
            avb = [self.sbx("avb0", (128, NSLOT), BF16),
                   self.sbx("avb1", (128, NSLOT), BF16)]
            wvdb = [self.sbx("wvdb0", (128, VBOT), BF16),
                    self.sbx("wvdb1", (128, VBOT), BF16)]
            p_avd = [self.psx("p_avd%d" % i, [128, 512]) for i in range(8)]
            av_cons = [None] * 32

            def avload(hc):
                p = hc % 2

                def _g(gq, hc=hc, p=p):
                    if hc >= 2:
                        sc.wait(gq, "POOL", av_cons[hc - 2])
                    sc.inc(gq.dma_start(
                        out=avb[p].ap(),
                        in_=self.dt["avT"].ap()[128 * hc:128 * (hc + 1), :]),
                        "d_av%d" % p, 16)
                    sc.inc(gq.dma_start(
                        out=wvdb[p].ap(),
                        in_=self.dt["wvdT"].ap()[128 * hc:128 * (hc + 1), :]),
                        "d_av%d" % p, 16)
                self.pool(_g)

            avload(0)
            avload(1)
            for hc in range(32):
                p = hc % 2
                ld = sc.tok("d_av%d" % p)

                def _t(t, hc=hc, p=p, ld=ld):
                    sc.wait(t, "PE", ld)
                    m = None
                    for nch in range(8):
                        m = t.matmul(p_avd[nch][:, 0:VBOT],
                                     avb[p][:, 128 * nch:128 * (nch + 1)],
                                     wvdb[p][:, :], start=(hc == 0), stop=(hc == 31))
                    sc.inc(m, "sp", 1)
                    av_cons[hc] = sc.tok("sp")
                self.pe(_t)
                if hc + 2 < 32:
                    avload(hc + 2)
            avd_mm = sc.tok("sp")

            def _v(v, avd_mm=avd_mm):
                sc.wait(v, "DVE", avd_mm)
                for nch in range(8):
                    sc.inc(v.tensor_copy(self.AVd_bf[:, nch, :], p_avd[nch][:, 0:VBOT]),
                           "sv", 1)
            self.dve(_v)
        avd_tok = sc.tok("sv")

        self.bar()
        with self.scope():
            rs_bf = self.sbx("rs_bf", (128, NSLOT), BF16)
            e_f = self.sbx("e_f", (128, NSLOT), F32)
            wm_bf = self.sbx("wm_bf", (128, NSLOT), BF16)
            w_bf = self.sbx("w_bf", (128, NSLOT), BF16)
            wT_bf = self.sbx("wT_bf", (128, 8, 128), BF16)
            top8 = self.sbx("top8", (128, 8), F32)
            acol = self.sbx("acol", (128, 4), F32)
            p_A = self.psx("p_A", [128, 1024])
            p_RS = self.psx("p_RS", [128, 1024])
            p_wT = self.psx("p_wT", [128, 512], BF16)
            p_ov = [self.psx("p_ov0", [128, 512]), self.psx("p_ov1", [128, 512])]

            prev_pa_v = None
            prev_pa_a = None
            for j in range(NOWN):
                def _t(t, j=j, prev_pa_v=prev_pa_v, prev_pa_a=prev_pa_a):
                    sc.wait(t, "PE", query_tok, self.rq_tok, self.cstg_tok,
                            prev_pa_v, prev_pa_a)
                    for nh in range(2):
                        t.matmul(p_A[:, 512 * nh:512 * (nh + 1)],
                                 self.query_bf[:, 128 * j:128 * (j + 1)],
                                 self.akT_bf[:, 512 * nh:512 * (nh + 1)],
                                 start=True, stop=False)
                        t.matmul(p_A[:, 512 * nh:512 * (nh + 1)], self.ones_row_b[:, :],
                                 self.logrel_bf[:, 512 * nh:512 * (nh + 1)],
                                 start=False, stop=True)
                    m = None
                    for nh in range(2):
                        t.matmul(p_RS[:, 512 * nh:512 * (nh + 1)],
                                 self.rq_bf[:, 128 * j:128 * (j + 1)],
                                 self.rkT_bf[:, 512 * nh:512 * (nh + 1)],
                                 start=True, stop=False)
                        m = t.matmul(p_RS[:, 512 * nh:512 * (nh + 1)],
                                     self.ones_row_b[:, :],
                                     self.logrel_bf[:, 512 * nh:512 * (nh + 1)],
                                     start=False, stop=True)
                    sc.inc(m, "sp", 1)
                self.pe(_t)
                mm_tok = sc.tok("sp")

                def _v(v, j=j, mm_tok=mm_tok):
                    sc.wait(v, "DVE", mm_tok)
                    sc.inc(v.tensor_copy(rs_bf[:, :], p_RS[:, :]), "sv", 1)
                    sc.inc(v.reduce_max(acol[:, 0:1], p_A[:, :], axis=AX.X,
                                        negate=True), "sv", 1)
                    sc.wait(v, "DVE", sc.tok("sv"))
                    sc.inc(v.max(top8[:, :], rs_bf[:, :]), "sv", 1)
                self.dve(_v)
                vtok = sc.tok("sv")
                prev_pa_v = vtok

                def _a(a, j=j, vtok=vtok):
                    sc.wait(a, "ACT", vtok)
                    sc.inc(a.activation(e_f[:, :], p_A[:, :], AF.Exp,
                                        bias=acol[:, 0:1]), "sa", 1)
                self.act(_a)
                atok = sc.tok("sa")
                prev_pa_a = atok

                def _v2(v, j=j, atok=atok):
                    sc.wait(v, "DVE", atok)
                    sc.inc(v.scalar_tensor_tensor(wm_bf[:, :], rs_bf[:, :],
                                                  top8[:, 7:8], e_f[:, :], ALU.is_ge,
                                                  ALU.mult, accum_out=acol[:, 1:2]),
                           "sv", 1)
                    sc.wait(v, "DVE", sc.tok("sv"))
                    sc.inc(v.reciprocal(acol[:, 2:3], acol[:, 1:2]), "sv", 1)
                    sc.wait(v, "DVE", sc.tok("sv"))
                    sc.inc(v.tensor_scalar(w_bf[:, :], wm_bf[:, :], acol[:, 2:3], None,
                                           ALU.mult), "sv", 1)
                self.dve(_v2)
                w_tok = sc.tok("sv")
                if self.debug and j == NOWN - 1:
                    def _sdbg(sq, w_tok=w_tok):
                        sc.wait(sq, "POOL", w_tok)
                        sc.inc(sq.dma_start(out=self.dbg["dbg_rs"].ap(),
                                            in_=rs_bf.ap()), "d_dbg", 16)
                        sc.inc(sq.dma_start(out=self.dbg["dbg_w"].ap(),
                                            in_=w_bf.ap()), "d_dbg", 16)
                        sq.wait_ge(self.sem["d_dbg"], self.cnt["d_dbg"])
                    self.pool(_sdbg)

                for nch in range(8):
                    def _t2(t, nch=nch, w_tok=w_tok):
                        sc.wait(t, "PE", w_tok, sc.tok("sv"))
                        sc.inc(t.transpose(p_wT[:, 0:128],
                                           w_bf[:, 128 * nch:128 * (nch + 1)],
                                           self.eye_b[:, :]), "sp", 1)
                    self.pe(_t2)
                    ttok = sc.tok("sp")

                    def _v3(v, nch=nch, ttok=ttok):
                        sc.wait(v, "DVE", ttok)
                        sc.inc(v.tensor_copy(wT_bf[:, nch, :], p_wT[:, 0:128]), "sv", 1)
                    self.dve(_v3)
                wT_tok = sc.tok("sv")

                def _t3(t, j=j, wT_tok=wT_tok):
                    sc.wait(t, "PE", wT_tok, avd_tok)
                    for vd in range(2):
                        self.mmacc(t, p_ov[vd][:, 0:128],
                                   [(self.AVd_bf[:, nch, 128 * vd:128 * (vd + 1)],
                                     wT_bf[:, nch, :]) for nch in range(8)])
                self.pe(_t3)
                ov_tok = sc.tok("sp")

                def _v4(v, j=j, ov_tok=ov_tok):
                    sc.wait(v, "DVE", ov_tok)
                    for vd in range(2):
                        sc.inc(v.tensor_copy(self.outv_bf[:, vd, 128 * j:128 * (j + 1)],
                                             p_ov[vd][:, 0:128]), "sv", 1)
                self.dve(_v4)
        self.outv_tok = sc.tok("sv")

        if self.debug:
            def _sdbg(sq):
                sc.wait(sq, "POOL", self.outv_tok, sc.tok("sv"))
                sc.inc(sq.dma_start(out=self.dbg["dbg_outv"].ap(),
                                    in_=self.outv_bf.ap()), "d_dbg", 16)
                sc.inc(sq.dma_start(out=self.dbg["dbg_query"].ap(),
                                    in_=self.query_bf.ap()), "d_dbg", 16)
                sq.wait_ge(self.sem["d_dbg"], self.cnt["d_dbg"])
            self.pool(_sdbg)

    def wu1_init(self):
        sc = self
        self.p_lr = self.psx("p_lr0", [1, 512])
        self.wu_wub = self.sbx("wub", (128, 32, 128), BF16)
        self.wu_rf = self.sbx("rf_f", (128, NTOK), F32)
        self.wu_tb = self.sbx("tb_bf", (128, NTOK), BF16)
        self.wu_z = self.sbx("z_bf", (128, NTOK), BF16)
        self.wu_g = self.sbx("g_bf", (128, NTOK), BF16)
        self.wu_pl = [self.psx("p_l0", [128, 512]), self.psx("p_l1", [128, 512])]
        self.wu_cons = [None] * 8
        self.wu_pl_war = None
        self.wu_lr_war = None
        self.wu_pending = None
        self.wu1_done = 0

        def wuload(uc):
            def _g(gq, uc=uc):
                if uc >= 1:
                    sc.wait(gq, "POOL", self.wu_cons[uc - 1])
                sc.inc(gq.dma_start(out=self.wu_wub.ap(),
                                    in_=self.dt["wu1T_t"].ap()[uc]), "d_wu0", 16)
            self.pool(_g)
        self._wuload = wuload
        wuload(0)

    def wu1_step(self):
        sc = self
        big = self.big_a
        uc = self.wu1_done
        self.wu1_done += 1
        wub, rf, tb, z_bf, g_bf, p_l = (self.wu_wub, self.wu_rf, self.wu_tb,
                                        self.wu_z, self.wu_g, self.wu_pl)
        if self.wu_pending is not None:
            self._wu_learned(*self.wu_pending)
            self.wu_pending = None
        ld = sc.tok("d_wu0")

        def _t(t, uc=uc, ld=ld, plwar=self.wu_pl_war):
            sc.wait(t, "PE", ld, self.xm_tok, plwar)
            for th in range(2):
                self.mmacc(t, p_l[th][:, :],
                           [(wub[:, c, :], big[:, c, 512 * th:512 * (th + 1)])
                            for c in range(32)])
            self.wu_cons[uc] = sc.tok("sp")
        self.pe(_t)
        l_tok = sc.tok("sp")
        if uc + 1 < 8:
            self._wuload(uc + 1)

        def _v(v, uc=uc, l_tok=l_tok):
            sc.wait(v, "DVE", l_tok)
            for th in range(2):
                sc.inc(v.tensor_tensor(rf[:, 512 * th:512 * (th + 1)],
                                       p_l[th][:, :],
                                       self.u_own_b[:, 512 * th:512 * (th + 1)],
                                       ALU.mult), "sv", 1)
            sc.wait(v, "DVE", sc.tok("sv"))
            sc.inc(v.tensor_scalar(z_bf[:, :], rf[:, :],
                                   self.bu1c_f[:, uc:uc + 1], None, ALU.add),
                   "sv", 1)
            self.wu_pl_war = sc.tok("sv")
            sc.wait(v, "DVE", sc.tok("sv"))
            sc.inc(v.tensor_tensor(tb[:, :], z_bf[:, :], z_bf[:, :], ALU.mult),
                   "sv", 1)
            sc.wait(v, "DVE", sc.tok("sv"))
            sc.inc(v.tensor_scalar(tb[:, :], tb[:, :], 2.0 * GELU_C2,
                                   2.0 * GELU_C1, ALU.mult, ALU.add), "sv", 1)
            sc.wait(v, "DVE", sc.tok("sv"))
            sc.inc(v.tensor_tensor(tb[:, :], z_bf[:, :], tb[:, :], ALU.mult),
                   "sv", 1)
        self.dve(_v)
        z2_tok = sc.tok("sv")

        def _a(a, uc=uc, z2_tok=z2_tok):
            sc.wait(a, "ACT", z2_tok)
            sc.inc(a.activation(rf[:, :], tb[:, :], AF.Exp, scale=-1.0), "sa", 1)
        self.act(_a)
        e_tok = sc.tok("sa")

        def _v2(v, uc=uc, e_tok=e_tok):
            sc.wait(v, "DVE", e_tok)
            sc.inc(v.tensor_scalar(rf[:, :], rf[:, :], 1.0, None, ALU.add), "sv", 1)
        self.dve(_v2)
        d1p_tok = sc.tok("sv")

        def _a2(a, uc=uc, d1p_tok=d1p_tok):
            sc.wait(a, "ACT", d1p_tok)
            sc.inc(a.activation(rf[:, :], rf[:, :], AF.Ln), "sa", 1)
            sc.wait(a, "ACT", sc.tok("sa"))
            sc.inc(a.activation(rf[:, :], rf[:, :], AF.Exp, scale=-1.0), "sa", 1)
        self.act(_a2)
        sig_tok = sc.tok("sa")

        def _v3(v, uc=uc, sig_tok=sig_tok):
            sc.wait(v, "DVE", sig_tok)
            sc.inc(v.tensor_tensor(g_bf[:, :], z_bf[:, :], rf[:, :], ALU.mult),
                   "sv", 1)
        self.dve(_v3)
        g_tok = sc.tok("sv")
        self.wu_pending = (uc, g_tok)

    def _wu_learned(self, uc, g_tok):
        sc = self
        g_bf = self.wu_g
        for th in range(2):
            def _t2(t, uc=uc, th=th, g_tok=g_tok, lr_war=self.wu_lr_war):
                sc.wait(t, "PE", g_tok, lr_war)
                sc.inc(t.matmul(self.p_lr[:, :], self.wu2c_b[:, uc:uc + 1],
                                g_bf[:, 512 * th:512 * (th + 1)],
                                start=True, stop=True), "sp", 1)
            self.pe(_t2)
            lr_tok = sc.tok("sp")

            def _vlr(v, uc=uc, th=th, lr_tok=lr_tok):
                sc.wait(v, "DVE", lr_tok)
                if uc == 0:
                    sc.inc(v.tensor_copy(
                        self.learned_row[0:1, 512 * th:512 * (th + 1)],
                        self.p_lr[0:1, :]), "sv", 1)
                else:
                    sc.inc(v.tensor_tensor(
                        self.learned_row[0:1, 512 * th:512 * (th + 1)],
                        self.learned_row[0:1, 512 * th:512 * (th + 1)],
                        self.p_lr[0:1, :], ALU.add), "sv", 1)
            self.dve(_vlr)
            self.wu_lr_war = sc.tok("sv")

    def wu1_finish(self):
        if self.wu_pending is not None:
            self._wu_learned(*self.wu_pending)
            self.wu_pending = None
        self.learned_tok = self.tok("sv")

    def p_b(self):
        sc = self
        big = self.big_a
        wqb = [self.sbx("wqb%d" % i, (128, 16, 128), BF16) for i in range(2)]
        p_qp = [self.psx("p_qp%d" % i, [128, 512]) for i in range(4)]
        wq_cons = [None] * 64
        wq_evac = [None] * 32
        self.wu1_init()

        def load(hb):
            blk, hf = divmod(hb, 2)
            p = hb % 2

            def _g(gq, hb=hb, blk=blk, hf=hf, p=p):
                if hb >= 2:
                    sc.wait(gq, "POOL", wq_cons[hb - 2])
                for g4 in range(4):
                    sc.inc(gq.dma_start(
                        out=wqb[p].ap()[:, 4 * g4:4 * (g4 + 1), :],
                        in_=self.dt["wq_t"].ap()[blk][
                            :, 16 * hf + 4 * g4:16 * hf + 4 * (g4 + 1), :]),
                        "d_wq%d" % (4 * p + g4), 16)
            self.pool(_g)

        load(0)
        load(1)
        for blk in range(32):
            def _t(t, blk=blk):
                bp = blk % 2
                sc.wait(t, "PE", self.xm_tok)
                if blk >= 2:
                    sc.wait(t, "PE", wq_evac[blk - 2])
                for hf in range(2):
                    hb = 2 * blk + hf
                    pb = hb % 2
                    for ci in range(16):
                        if ci % 4 == 0:
                            sc.wait(t, "PE", ("d_wq%d" % (4 * pb + ci // 4),
                                              16 * (blk + 1)))
                        c = 16 * hf + ci
                        for th in range(2):
                            m = t.matmul(
                                p_qp[2 * bp + th][:, :], wqb[pb][:, ci, :],
                                big[:, c, 512 * th:512 * (th + 1)],
                                start=(c == 0), stop=(c == 31))
                    sc.inc(m, "sp", 1)
                    wq_cons[2 * blk + hf] = sc.tok("sp")
            self.pe(_t)
            mm_tok = sc.tok("sp")

            def _v(v, blk=blk, mm_tok=mm_tok):
                bp = blk % 2
                sc.wait(v, "DVE", mm_tok)
                for th in range(2):
                    sc.inc(v.tensor_tensor(
                        self.q_bf[:, blk, 512 * th:512 * (th + 1)],
                        p_qp[2 * bp + th][:, :],
                        self.u_own_b[:, 512 * th:512 * (th + 1)], ALU.mult), "sv", 1)
                wq_evac[blk] = sc.tok("sv")
            self.dve(_v)
            if blk % 2 == 0:
                self.feed_step()
            if blk % 4 == 3:
                self.wu1_step()
            for hb in (2 * blk + 2, 2 * blk + 3):
                if hb < 64:
                    load(hb)
        self.q_tok = sc.tok("sv")
        while self.wu1_done < 8:
            self.wu1_step()
        self.wu1_finish()
        if self.debug:
            def _sdbg(sq):
                sc.wait(sq, "POOL", self.q_tok)
                sc.inc(sq.dma_start(out=self.dbg["dbg_q"].ap(),
                                    in_=self.q_bf.ap()[:, 0:4, :]), "d_dbg", 16)
                sq.wait_ge(self.sem["d_dbg"], self.cnt["d_dbg"])
            self.pool(_sdbg)

    def p_c(self):
        sc = self
        big = self.big_a
        with self.scope():
            self.masks_sb = self.sbx("masks_sb", (128, 8, 2, 128), BF16)

            def _sld(sq):
                sc.inc(sq.dma_start(out=self.masks_sb.ap(),
                                    in_=self.dt["masks"].ap()), "d_cst", 16)
            self.syn(_sld)
            self.cst_tok = sc.tok("d_cst")
            negmg_col = self.sbx("negmg_col", (128, 1), F32)
            q2_bf = self.sbx("q2_bf", (128, NTOK), BF16)
            kcol = self.sbx("kcol", (1, 4), F32)
            with self.scope():
                k2_bf = self.sbx("k2_bf", (128, T), BF16)
                krow = self.sbx("krow", (1, 512), F32)
                kmx4 = self.sbx("kmx4", (1, 4), F32)
                p_k2 = self.psx("p_k2", [1, 512])

                def _v(v):
                    sc.wait(v, "DVE", self.kv_done, sc.tok("sp"))
                    sc.inc(v.tensor_tensor(k2_bf[:, :], self.K_bf[:, :],
                                           self.K_bf[:, :], ALU.mult), "sv", 1)
                self.dve(_v)
                k2_tok = sc.tok("sv")

                for kh in range(4):
                    def _t(t, kh=kh, k2_tok=k2_tok):
                        sc.wait(t, "PE", k2_tok, sc.tok("sv"))
                        sc.inc(t.matmul(p_k2[:, :], self.ones_col_b[:, :],
                                        k2_bf[:, 512 * kh:512 * (kh + 1)],
                                        start=True, stop=True), "sp", 1)
                    self.pe(_t)
                    ktok = sc.tok("sp")

                    def _v1(v, kh=kh, ktok=ktok):
                        sc.wait(v, "DVE", ktok)
                        sc.inc(v.tensor_copy(krow[0:1, :], p_k2[:, :]), "sv", 1)
                        sc.wait(v, "DVE", sc.tok("sv"))
                        sc.inc(v.reduce_max(kmx4[0:1, kh:kh + 1], krow[0:1, :],
                                            axis=AX.X), "sv", 1)
                    self.dve(_v1)

                def _v2(v):
                    sc.wait(v, "DVE", sc.tok("sv"))
                    sc.inc(v.reduce_max(kcol[0:1, 0:1], kmx4[0:1, :], axis=AX.X),
                           "sv", 1)
                self.dve(_v2)
                km_tok = sc.tok("sv")

                def _a(a, km_tok=km_tok):
                    sc.wait(a, "ACT", km_tok)
                    sc.inc(a.activation(kcol[0:1, 1:2], kcol[0:1, 0:1], AF.Ln),
                           "sa", 1)
                    sc.wait(a, "ACT", sc.tok("sa"))
                    sc.inc(a.activation(kcol[0:1, 2:3], kcol[0:1, 1:2], AF.Exp,
                                        scale=0.5), "sa", 1)
                self.act(_a)
                km2_tok = sc.tok("sa")

                def _v3(v, km2_tok=km2_tok):
                    sc.wait(v, "DVE", km2_tok)
                    sc.inc(v.tensor_scalar(kcol[0:1, 3:4], kcol[0:1, 2:3], -1.05,
                                           None, ALU.mult), "sv", 1)
                self.dve(_v3)
                kneg_tok = sc.tok("sv")

            kneg_tok2 = sc.tok("sv")

            PT_bf = self.sbx("PT_bf", (128, 2560), BF16)
            rsb = self.sbx("rsb", (128, 512), F32)
            rscr = self.sbx("rscr", (128, 512), F32)
            p_pair = [self.psx("p_pr%d" % i, [128, 1024]) for i in range(2)]
            p_av = self.psx("p_av", [128, 512])
            p_sum = self.psx("p_sum", [128, 512])
            p_nm = self.psx("p_nm", [128, 512])

            def chunks_for(qh):
                base = 512 * qh
                out = []
                for c in range(16):
                    qo = max(base, 128 * (c // 2))
                    wdt = base + 512 - qo
                    if wdt <= 0 or wdt > 512:
                        wdt = min(wdt, 512)
                    if wdt <= 0:
                        continue
                    out.append((c, qo, wdt))
                return out

            av_evac = None
            pend = None
            pair_evac = [None, None]
            prev_ntok = None
            prev_nmread = None
            PT2 = [PT_bf, self.sbx("PT_bf2", (128, 6656), BF16)]

            def emit_av(st):
                h, qh, chs, off, exp_done, PTb = st
                base = 512 * qh

                def _t2(t, av_war=av_evac):
                    sc.wait(t, "PE", exp_done, av_war)
                    n = len(chs)
                    for (i, (c, qo, wdt)) in enumerate(chs):
                        t.matmul(p_av[:, qo - base:512], self.V_bf[:, c, :],
                                 PTb[:, off[c]:off[c] + wdt],
                                 start=(i == 0), stop=(i == n - 1))
                    m = None
                    for (i, (c, qo, wdt)) in enumerate(chs):
                        m = t.matmul(p_sum[:, qo - base:512],
                                     self.ones_mat_b[:, :],
                                     PTb[:, off[c]:off[c] + wdt],
                                     start=(i == 0), stop=(i == n - 1))
                    sc.inc(m, "sp", 1)
                self.pe(_t2)
                av_tok = sc.tok("sp")

                def _a6(a, av_tok=av_tok):
                    sc.wait(a, "ACT", av_tok, sc.tok("sa"))
                    sc.inc(a.activation(rscr[:, :], p_sum[:, :], AF.Ln), "sa", 1)
                    sc.wait(a, "ACT", sc.tok("sa"))
                    sc.inc(a.activation(rsb[:, :], rscr[:, :], AF.Exp, scale=-1.0),
                           "sa", 1)
                self.act(_a6)

                def _v7(v, h=h, qh=qh, rtok=sc.tok("sa")):
                    sc.wait(v, "DVE", rtok)
                    sc.inc(v.tensor_tensor(big[:, h, 512 * qh:512 * (qh + 1)],
                                           p_av[:, :], rsb[:, :], ALU.mult),
                           "sv", 1)
                self.dve(_v7)
                self.feed_step()
                return sc.tok("sv")

            for h in range(H):
                def _vq2(v, h=h, prev_ntok=prev_ntok):
                    sc.wait(v, "DVE", self.q_tok, prev_ntok)
                    sc.inc(v.tensor_tensor(q2_bf[:, :], self.q_bf[:, h, :],
                                           self.q_bf[:, h, :], ALU.mult), "sv", 1)
                self.dve(_vq2)
                q2_tok = sc.tok("sv")

                nm_war = prev_nmread
                for th2 in range(2):
                    def _tq2(t, h=h, th2=th2, q2_tok=q2_tok, nm_war=nm_war):
                        sc.wait(t, "PE", q2_tok, nm_war)
                        sc.inc(t.matmul(p_nm[:, :], self.ones_mat_b[:, :],
                                        q2_bf[:, 512 * th2:512 * (th2 + 1)],
                                        start=True, stop=True), "sp", 1)
                    self.pe(_tq2)
                    ntok = sc.tok("sp")

                    def _vq3(v, th2=th2, ntok=ntok):
                        sc.wait(v, "DVE", ntok)
                        sc.inc(v.reduce_max(kcol[0:1, th2:th2 + 1], p_nm[0:1, :],
                                            axis=AX.X), "sv", 1)
                    self.dve(_vq3)
                    nm_war = sc.tok("sv")

                prev_ntok = sc.tok("sp")

                def _vq4(v):
                    sc.wait(v, "DVE", sc.tok("sv"))
                    sc.inc(v.tensor_tensor(kcol[0:1, 1:2], kcol[0:1, 0:1],
                                           kcol[0:1, 1:2], ALU.max), "sv", 1)
                self.dve(_vq4)
                mx_tok = sc.tok("sv")

                def _aq3(a, mx_tok=mx_tok):
                    sc.wait(a, "ACT", mx_tok, kneg_tok2)
                    sc.inc(a.activation(kcol[0:1, 0:1], kcol[0:1, 1:2], AF.Ln),
                           "sa", 1)
                    sc.wait(a, "ACT", sc.tok("sa"))
                    sc.inc(a.activation(kcol[0:1, 1:2], kcol[0:1, 0:1], AF.Exp,
                                        scale=0.5), "sa", 1)
                self.act(_aq3)
                mag_tok = sc.tok("sa")

                def _vnm(v, mag_tok=mag_tok):
                    sc.wait(v, "DVE", mag_tok)
                    sc.inc(v.tensor_scalar(kcol[0:1, 2:3], kcol[0:1, 1:2],
                                           kcol[0:1, 3:4], -SQRT_DH,
                                           ALU.mult, ALU.add), "sv", 1)
                    sc.wait(v, "DVE", sc.tok("sv"))
                    sc.inc(v.tensor_scalar(kcol[0:1, 2:3], kcol[0:1, 2:3], S_ATT,
                                           None, ALU.mult), "sv", 1)
                self.dve(_vnm)
                sct_tok = sc.tok("sv")

                def _tnb(t, sct_tok=sct_tok):
                    sc.wait(t, "PE", sct_tok)
                    sc.inc(t.matmul(p_nm[:, 0:1], self.ones_row_f[:, :],
                                    kcol[0:1, 2:3], start=True, stop=True), "sp", 1)
                self.pe(_tnb)
                nb_tok = sc.tok("sp")

                def _vnb(v, nb_tok=nb_tok, pe1=pair_evac[0], pe2=pair_evac[1]):
                    sc.wait(v, "DVE", nb_tok, pe1, pe2)
                    sc.inc(v.tensor_copy(negmg_col[:, :], p_nm[:, 0:1]), "sv", 1)
                self.dve(_vnb)
                nm_tok = sc.tok("sv")
                prev_nmread = nm_tok

                for qh in range(2):
                    PTb = PT2[qh]
                    base = 512 * qh
                    chs = chunks_for(qh)
                    off = {}
                    run = 0
                    for (c, qo, wdt) in chs:
                        off[c] = run
                        run += wdt
                    for pi in range(0, len(chs), 2):
                        par = (pi // 2) % 2
                        pair = chs[pi:pi + 2]

                        def _t1(t, h=h, qh=qh, pair=pair, par=par, nm_tok=nm_tok,
                                pe_war=pair_evac[par]):
                            sc.wait(t, "PE", self.q_tok, self.kv_done, nm_tok,
                                    self.cst_tok, pe_war)
                            for (ci, (c, qo, wdt)) in enumerate(pair):
                                psl = p_pair[par][:, 512 * ci:512 * ci + wdt]
                                js = c // 2
                                has_mask = (128 * js >= base)
                                m = t.matmul(psl,
                                             self.K_bf[:, 128 * c:128 * (c + 1)],
                                             self.q_bf[:, h, qo:qo + wdt],
                                             start=True, stop=(not has_mask))
                                if has_mask:
                                    m = t.matmul(
                                        p_pair[par][:, 512 * ci:512 * ci + 128],
                                        self.eye_b[:, :],
                                        self.masks_sb[:, js, c % 2, :],
                                        start=False, stop=True)
                            sc.inc(m, "sp", 1)
                        self.pe(_t1)
                        st_tok = sc.tok("sp")

                        def _a1(a, pair=pair, par=par, st_tok=st_tok, off=off,
                                PTb=PTb):
                            sc.wait(a, "ACT", st_tok)
                            (c0, qo0, w0) = pair[0]
                            if len(pair) == 2:
                                srcv = p_pair[par][:, 0:1024]                                     .rearrange("p (i w) -> p i w", i=2)[:, :, 0:w0]
                                dst = PTb[:, off[c0]:off[c0] + 2 * w0]                                     .rearrange("p (i w) -> p i w", i=2)
                                sc.inc(a.activation(dst, srcv, AF.Exp, scale=S_ATT,
                                                    bias=negmg_col[:, :]), "sa", 1)
                            else:
                                sc.inc(a.activation(PTb[:, off[c0]:off[c0] + w0],
                                                    p_pair[par][:, 0:w0], AF.Exp,
                                                    scale=S_ATT,
                                                    bias=negmg_col[:, :]), "sa", 1)
                        self.act(_a1)
                        pair_evac[par] = sc.tok("sa")

                    st = (h, qh, chs, off, sc.tok("sa"), PTb)
                    if pend is not None:
                        av_evac = emit_av(pend)
                    pend = st
            av_evac = emit_av(pend)
        self.att_tok = sc.tok("sv")
        if self.debug:
            def _sdbg(sq):
                sc.wait(sq, "POOL", self.att_tok)
                sc.inc(sq.dma_start(out=self.dbg["dbg_att"].ap(),
                                    in_=big.ap()[:, 0:4, :]), "d_dbg", 16)
                sq.wait_ge(self.sem["d_dbg"], self.cnt["d_dbg"])
            self.pool(_sdbg)

    def p_d(self):
        sc = self
        big = self.big_a
        self.u1_b = self.sbx("u1_b", (128, NTOK), F32)
        self.mu_b = self.sbx("mu_b", (128, NTOK), F32)
        p_exd = self.psx("p_exd", [128, 512])
        self.ebcast(p_exd, "u1", NOWN, True, self.u1_b)
        self.ebcast(p_exd, "mu", NOWN, True, self.mu_b)
        umu_tok = sc.tok("sv")
        wvuT_sb = self.sbx("wvuT_sb", (128, 2, D), BF16)
        gate_b = self.sbx("gate_b", (128, NTOK), F32)
        wob = [self.sbx("wob0", (128, 32, 128), BF16),
               self.sbx("wob1", (128, 32, 128), BF16)]
        xfb = [self.sbx("xfb0", (128, NTOK), F32),
               self.sbx("xfb1", (128, NTOK), F32)]
        auxg = self.sbx("auxg", (128, NTOK), F32)
        outst = [self.sbx("outst0", (128, NTOK), F32),
                 self.sbx("outst1", (128, NTOK), F32)]
        p_wo = [self.psx("p_wo%d" % i, [128, 512]) for i in range(4)]
        p_ax = [self.psx("p_ax%d" % i, [128, 512]) for i in range(2)]
        p_gb = self.psx("p_gb", [128, 512])

        def _g(gq):
            sc.inc(gq.dma_start(out=wvuT_sb.ap(), in_=self.dt["wvuT_t"].ap()),
                   "d_wvd", 16)
        self.pool(_g)
        wvu_tok = sc.tok("d_wvd")

        for th in range(2):
            def _t(t, th=th):
                sc.wait(t, "PE", self.gate_tok, sc.tok("sv"))
                sc.inc(t.matmul(p_gb[:, :], self.ones_row_f[:, :],
                                self.gate_row[0:1, 512 * th:512 * (th + 1)],
                                start=True, stop=True), "sp", 1)
            self.pe(_t)
            gtok = sc.tok("sp")

            def _v(v, th=th, gtok=gtok):
                sc.wait(v, "DVE", gtok)
                sc.inc(v.tensor_copy(gate_b[:, 512 * th:512 * (th + 1)], p_gb[:, :]),
                       "sv", 1)
            self.dve(_v)
        gb_tok = sc.tok("sv")

        wo_cons = [None] * 32
        out_done = [None] * 32
        xf_cons = [None] * 32
        ax_done = [None] * 32
        st_sent = [None] * 32

        def load(dc):
            p = dc % 2

            def _g(gq, dc=dc, p=p):
                if dc >= 2:
                    sc.wait(gq, "POOL", wo_cons[dc - 2])
                sc.inc(gq.dma_start(out=wob[p].ap(), in_=self.dt["wo_t"].ap()[dc]),
                       "d_wo%d" % p, 16)
            self.pool(_g)

            def _s(sq, dc=dc, p=p):
                if dc >= 2:
                    sc.wait(sq, "SYNC", xf_cons[dc - 2])
                sc.inc(sq.dma_start(out=xfb[p].ap(),
                                    in_=self.dt["xmT"].ap()[128 * dc:128 * (dc + 1), :]),
                       "d_xf%d" % p, 16)
            self.syn(_s)

        load(0)
        load(1)
        for dc in range(32):
            p = dc % 2
            wo_ld = sc.tok("d_wo%d" % p)
            xf_ld = sc.tok("d_xf%d" % p)

            def _t(t, dc=dc, p=p, wo_ld=wo_ld):
                sc.wait(t, "PE", wo_ld, self.att_tok, self.outv_tok, wvu_tok)
                if dc >= 2:
                    sc.wait(t, "PE", out_done[dc - 2])
                if dc >= 1:
                    sc.wait(t, "PE", ax_done[dc - 1])
                for th in range(2):
                    self.mmacc(t, p_wo[2 * p + th][:, :],
                               [(wob[p][:, c, :], big[:, c, 512 * th:512 * (th + 1)])
                                for c in range(32)])
                wo_cons[dc] = sc.tok("sp")
                for th in range(2):
                    self.mmacc(t, p_ax[th][:, :],
                               [(wvuT_sb[:, vc, 128 * dc:128 * (dc + 1)],
                                 self.outv_bf[:, vc, 512 * th:512 * (th + 1)])
                                for vc in range(2)])
            self.pe(_t)
            mm_tok = sc.tok("sp")

            def _v(v, dc=dc, mm_tok=mm_tok):
                sc.wait(v, "DVE", mm_tok, gb_tok)
                for th in range(2):
                    sl = slice(512 * th, 512 * (th + 1))
                    sc.inc(v.tensor_tensor(auxg[:, sl], p_ax[th][:, :], gate_b[:, sl],
                                           ALU.mult), "sv", 1)
                ax_done[dc] = sc.tok("sv")
            self.dve(_v)
            axv_tok = sc.tok("sv")

            def _p(g, dc=dc, p=p, axv_tok=axv_tok, xf_ld=xf_ld):
                sc.wait(g, "POOL", axv_tok, xf_ld, umu_tok)
                if dc >= 2:
                    sc.wait(g, "POOL", st_sent[dc - 2])
                sc.inc(g.tensor_tensor(outst[p][:, :], xfb[p][:, :], self.u1_b[:, :],
                                       ALU.mult), "sg", 1)
                sc.wait(g, "POOL", sc.tok("sg"))
                sc.inc(g.tensor_tensor(outst[p][:, :], outst[p][:, :], self.mu_b[:, :],
                                       ALU.subtract), "sg", 1)
                sc.wait(g, "POOL", sc.tok("sg"))
                sc.inc(g.tensor_tensor(outst[p][:, :], outst[p][:, :], auxg[:, :],
                                       ALU.add), "sg", 1)
                xf_cons[dc] = sc.tok("sg")
            self.pool(_p)
            pl_tok = sc.tok("sg")

            def _v2(v, dc=dc, p=p, pl_tok=pl_tok, mm_tok=mm_tok):
                sc.wait(v, "DVE", pl_tok, mm_tok)
                for th in range(2):
                    sl = slice(512 * th, 512 * (th + 1))
                    sc.inc(v.tensor_tensor(outst[p][:, sl], outst[p][:, sl],
                                           p_wo[2 * p + th][:, :], ALU.add), "sv", 1)
                out_done[dc] = sc.tok("sv")
            self.dve(_v2)
            st_tok = sc.tok("sv")

            def _s(sq, dc=dc, p=p, st_tok=st_tok):
                sc.wait(sq, "SYNC", st_tok)
                st_sent[dc] = sc.inc(
                    sq.dma_start(out=self.outT.ap()[128 * dc:128 * (dc + 1), :],
                                 in_=outst[p][:, :]), "d_o%d" % p, 16)
            self.syn(_s)
            if dc + 2 < 32:
                load(dc + 2)

    def finish(self):
        def _s(sq):
            sq.wait_ge(self.sem["d_o0"], self.cnt["d_o0"])
            sq.wait_ge(self.sem["d_o1"], self.cnt["d_o1"])
        self.syn(_s)
        if self.cnt["d_dbg"]:
            def _g(gq):
                gq.wait_ge(self.sem["d_dbg"], self.cnt["d_dbg"])
            self.pool(_g)



def _host_core_inputs(core, inputs):
    b = core % 4
    e = core // 4
    own_tiles = [2 * j + e for j in range(NOWN)]

    x_full = np.asarray(inputs["L0_attn_x"])
    x = x_full[b]
    own_rows = np.concatenate([np.arange(128 * t, 128 * (t + 1)) for t in own_tiles])
    xm = x[own_rows]

    wq = np.asarray(inputs["L0_attn_wq"])
    wo = np.asarray(inputs["L0_attn_wo"])
    wk = np.asarray(inputs["L0_attn_wk"])
    wv = np.asarray(inputs["L0_attn_wv"])
    Wq = np.asarray(inputs["L0_ffn_Wq"])
    Wr = np.asarray(inputs["L0_ffn_Wr"])
    ak = np.asarray(inputs["L0_ffn_aux_keys"])
    av = np.asarray(inputs["L0_ffn_aux_values"])
    Wvd = np.asarray(inputs["L0_ffn_Wvd"])
    Wvu = np.asarray(inputs["L0_ffn_Wvu"])
    Wu1 = np.asarray(inputs["L0_ffn_Wu1"])
    bu1 = np.asarray(inputs["L0_ffn_bu1"])
    Wu2 = np.asarray(inputs["L0_ffn_Wu2"])
    bu2 = np.asarray(inputs["L0_ffn_bu2"])
    gw = np.asarray(inputs["L0_ffn_gate_w1"])
    gb = np.asarray(inputs["L0_ffn_gate_bias"])
    lr = np.asarray(inputs["L0_ffn_log_rel"])

    d = {}
    batch_order = [(b + k) % 4 for k in range(4)]
    d["x_all_t"] = np.ascontiguousarray(
        x_full[batch_order].reshape(4, NT, 128, 32, 128).transpose(0, 1, 4, 3, 2))
    d["xm_t"] = np.ascontiguousarray(
        xm.reshape(NOWN, 128, 32, 128).transpose(3, 2, 0, 1).reshape(128, 32, NTOK))
    d["xmT"] = np.ascontiguousarray(xm.T)
    d["wq_t"] = np.ascontiguousarray(wq.reshape(32, 128, 32, 128).transpose(2, 1, 0, 3))
    d["wo_t"] = np.ascontiguousarray(wo.reshape(32, 128, 32, 128).transpose(2, 1, 0, 3))
    d["wk_t"] = np.ascontiguousarray(wk.reshape(32, 128, 128).transpose(1, 0, 2))
    d["wv_t"] = np.ascontiguousarray(wv.reshape(32, 128, 128).transpose(1, 0, 2))
    d["wqf_t"] = np.ascontiguousarray(Wq.T.reshape(32, 128, BOT).transpose(1, 0, 2))
    d["wr_T"] = np.ascontiguousarray(Wr.T)
    d["akT"] = np.ascontiguousarray(ak.T)
    d["avT"] = np.ascontiguousarray(av.T)
    d["wvdT"] = np.ascontiguousarray(Wvd.T)
    d["wvuT_t"] = np.ascontiguousarray(Wvu.T.reshape(2, 128, D).transpose(1, 0, 2))
    d["wu1T_t"] = np.ascontiguousarray(Wu1.reshape(8, 128, 32, 128).transpose(0, 3, 2, 1))
    d["wu2c"] = np.ascontiguousarray(Wu2.reshape(8, 128).T)
    d["bu1c"] = np.ascontiguousarray(bu1.reshape(8, 128).T)
    d["logrel_r"] = np.ascontiguousarray(lr[None, :])
    d["scal"] = np.array([[float(gw), float(gb), float(np.reshape(bu2, (-1,))[0]), 0.0]],
                         dtype=np.float32)

    masks = np.zeros((128, NOWN, 2, 128), dtype=np.float32)
    kp = np.arange(128)
    qi = np.arange(128)
    for js in range(NOWN):
        tj = own_tiles[js]
        for hh in range(2):
            kt_pos = 256 * js + 128 * hh + kp[:, None]
            q_pos = 128 * tj + qi[None, :]
            masks[:, js, hh, :] = np.where(kt_pos <= q_pos, 0.0, -1.0e9)
    d["masks"] = masks.astype(BFNP)


    d["c_ones_row_b"] = np.ones((1, 128), dtype=BFNP)
    d["c_ones_col_b"] = np.ones((128, 1), dtype=BFNP)
    d["c_ones_row_f"] = np.ones((1, 128), dtype=np.float32)
    d["c_ones_mat_b"] = np.ones((128, 128), dtype=BFNP)
    d["c_eye_b"] = np.eye(128, dtype=np.float32).astype(BFNP)
    d["c_eye_f"] = np.eye(128, dtype=np.float32)
    d["c_sel16_f"] = np.eye(16, dtype=np.float32)
    selown = np.zeros((16, 8), dtype=np.float32)
    for j in range(NOWN):
        selown[own_tiles[j], j] = 1.0
    d["c_selown_f"] = selown
    for k in list(d):
        if d[k].dtype == np.float64:
            d[k] = d[k].astype(np.float32)
    return d


_CACHE = {}


def kernel(**inputs):
    if "nc" not in _CACHE:
        _CACHE["nc"] = Builder(debug=False).build()
    nc = _CACHE["nc"]
    in_maps = [_host_core_inputs(c, inputs) for c in range(8)]
    res = run_bass_kernel_spmd(nc, in_maps, core_ids=list(range(8)))
    out = np.zeros((B, T, D), dtype=np.float32)
    for c in range(8):
        b = c % 4
        e = c // 4
        own_tiles = [2 * j + e for j in range(NOWN)]
        o = np.asarray(res.results[c]["outT"]).T
        for j, t in enumerate(own_tiles):
            out[b, 128 * t:128 * (t + 1), :] = o[128 * j:128 * (j + 1), :]
    return out


# version 20 (final)
# speedup vs baseline: 1.1004x; 1.1004x over previous
import numpy as np
import ml_dtypes
from contextlib import ExitStack

import concourse.bass as bass
from concourse import dve_ops
import concourse.mybir as mybir
from concourse.bass_utils import run_bass_kernel_spmd

AF = mybir.ActivationFunctionType
ALU = mybir.AluOpType
AX = mybir.AxisListType
F32 = mybir.dt.float32
BF16 = mybir.dt.bfloat16
BFNP = ml_dtypes.bfloat16

B, T, D = 4, 2048, 4096
H, DH = 32, 128
NT = 16
NOWN = 8
NTOK = NOWN * 128
BOT, RDIM, NSLOT, KTOP, VBOT = 64, 48, 1024, 8, 256
EPS = 1e-5
S_ATT = 1.0 / (DH ** 0.5)
SQRT_DH = float(DH ** 0.5)
GELU_C1 = 0.7978845608028654
GELU_C2 = GELU_C1 * 0.044715


class Builder:
    def __init__(self, debug=False):
        self.debug = debug
        self.nc = bass.Bass("TRN2", target_bir_lowering=False)
        self.es = ExitStack()
        self.pes = None
        self.cnt = {}
        self.sem = {}
        self.memo = {}
        self.dt = {}
        self.dbg = {}

    def mksem(self, *names):
        for n in names:
            self.sem[n] = self.es.enter_context(self.nc.semaphore(n))
            self.cnt[n] = 0

    def inc(self, inst, name, n=1):
        self.cnt[name] += n
        inst.then_inc(self.sem[name], n)
        return (name, self.cnt[name])

    def tok(self, name):
        return (name, self.cnt[name])

    def wait(self, eng, engname, *toks):
        for tk in toks:
            if tk is None:
                continue
            name, v = tk
            if v <= 0:
                continue
            key = (engname, name)
            if self.memo.get(key, 0) >= v:
                continue
            eng.wait_ge(self.sem[name], v)
            self.memo[key] = v

    def pe(self, fn):
        self.block.tensor(lambda t: fn(t))

    def dve(self, fn):
        self.block.vector(lambda v: fn(v))

    def act(self, fn):
        self.block.scalar(lambda a: fn(a))

    def pool(self, fn):
        self.block.gpsimd(lambda g: fn(g))

    def syn(self, fn):
        self.block.sync(lambda s: fn(s))

    def sb(self, name, shape, dtype, side="left"):
        return self.es.enter_context(
            self.nc.sbuf_tensor(name, list(shape), dtype, side=side))

    def sbx(self, name, shape, dtype, side="right"):
        return self.pes.enter_context(
            self.nc.sbuf_tensor(name, list(shape), dtype, side=side))

    def psx(self, name, shape, dtype=F32):
        return self.pes.enter_context(self.nc.psum_tensor(name, list(shape), dtype))

    def scope(self):
        st = ExitStack()
        old = self.pes
        self.pes = st

        class _G:
            def __enter__(sg):
                return sg

            def __exit__(sg, *a):
                st.close()
                self.pes = old
                return False
        return _G()

    def din(self, name, shape, dtype=F32):
        self.dt[name] = self.nc.dram_tensor(name, list(shape), dtype, kind="ExternalInput")
        return self.dt[name]

    def ebcast(self, p_ex, qn, ntiles, own, dest, dest_row=None):
        sc = self
        for j in range(ntiles):
            selap = self.selown_f[:, j:j + 1] if own else self.sel16_f[:, j:j + 1]

            def _t(t, selap=selap, qn=qn):
                sc.wait(t, "PE", self.strows_tok, self.cst_tok, sc.tok("sv"))
                sc.inc(t.matmul(p_ex[0:1, 0:128], selap, self.stat_rows[qn][:, :],
                                start=True, stop=True), "sp", 1)
            self.pe(_t)
            etok = sc.tok("sp")

            def _v(v, j=j, etok=etok):
                sc.wait(v, "DVE", etok)
                sc.inc(v.tensor_copy(self.row_scr[:, :], p_ex[0:1, 0:128]), "sv", 1)
                if dest_row is not None:
                    sc.inc(v.tensor_copy(dest_row[0:1, 128 * j:128 * (j + 1)],
                                         p_ex[0:1, 0:128]), "sv", 1)
            self.dve(_v)
            rtok = sc.tok("sv")
            if dest is not None:
                def _t2(t, rtok=rtok):
                    sc.wait(t, "PE", rtok)
                    sc.inc(t.matmul(p_ex[:, 128:256], self.ones_row_f[:, :],
                                    self.row_scr[:, :], start=True, stop=True),
                           "sp", 1)
                self.pe(_t2)
                btok = sc.tok("sp")

                def _v2(v, j=j, btok=btok):
                    sc.wait(v, "DVE", btok)
                    sc.inc(v.tensor_copy(dest[:, 128 * j:128 * (j + 1)],
                                         p_ex[:, 128:256]), "sv", 1)
                self.dve(_v2)

    def feed_init(self):
        sc = self
        self.f_xkv = [self.sbx("f_xkv0", (128, 32, 128), BF16),
                      self.sbx("f_xkv1", (128, 32, 128), BF16)]
        self.f_gd = self.sbx("f_gd", (128, 128), F32)
        self.f_cols = self.sbx("f_cols", (128, 6), F32)
        self.p_st = self.psx("p_st", [128, 512])
        self.f_n = 48
        self.f_loaded = 0
        self.f_done = 0
        self.f_cons = [None] * self.f_n
        self.f_pst_war = None
        self.f_pst_war_new = None

        def loadk(k):
            b4, t_i = divmod(k + NT, NT)
            p = k % 2

            def _g(gq, k=k, b4=b4, t_i=t_i, p=p):
                if k >= 2:
                    sc.wait(gq, "POOL", self.f_cons[k - 2])
                sc.inc(gq.dma_start(out=self.f_xkv[p].ap(),
                                    in_=self.dt["x_all_t"].ap()[b4, t_i]),
                       "d_kv%d" % p, 16)
            self.pool(_g)
            self.f_loaded += 1
        self._feed_loadk = loadk
        loadk(0)
        loadk(1)

    def feed_step(self):
        sc = self
        if self.f_done >= self.f_n:
            return
        k = self.f_done
        self.f_done += 1
        p = k % 2
        ld = sc.tok("d_kv%d" % p)
        FS1, FS2, FM, FVAR, FLNV, FTMP = range(6)

        def fcol(q):
            return self.f_cols[:, q:q + 1]

        def _t(t, k=k, p=p, ld=ld, pst_war=self.f_pst_war):
            sc.wait(t, "PE", ld, self.cst_tok, pst_war)
            self.mmacc(t, self.p_st[:, 0:128],
                       [(self.f_xkv[p][:, c, :], self.f_xkv[p][:, c, :])
                        for c in range(32)])
        self.pe(_t)
        gm_tok = sc.tok("sp")

        def _v(v, gm_tok=gm_tok):
            sc.wait(v, "DVE", gm_tok)
            sc.inc(v.tensor_tensor(self.f_gd[:, :], self.p_st[:, 0:128],
                                   self.eye_f[:, :], ALU.mult), "sv", 1)
            sc.wait(v, "DVE", sc.tok("sv"))
            sc.inc(v.reduce_sum(fcol(FS2), self.f_gd[:, :], axis=AX.X), "sv", 1)
        self.dve(_v)
        gd_tok = sc.tok("sv")

        def _t2(t, k=k, p=p, gd_tok=gd_tok):
            sc.wait(t, "PE", gd_tok)
            self.mmacc(t, self.p_st[:, 128:129],
                       [(self.f_xkv[p][:, c, :], self.ones_col_b[:, :])
                        for c in range(32)])
            self.f_cons[k] = sc.tok("sp")
        self.pe(_t2)
        cs_tok = sc.tok("sp")

        def _v2(v, cs_tok=cs_tok):
            sc.wait(v, "DVE", cs_tok)
            sc.inc(v.tensor_scalar(fcol(FM), self.p_st[:, 128:129], 1.0 / D, None,
                                   ALU.mult), "sv", 1)
            self.f_pst_war_new = sc.tok("sv")
            sc.inc(v.tensor_scalar(fcol(FVAR), fcol(FS2), 1.0 / D, None, ALU.mult),
                   "sv", 1)
            sc.wait(v, "DVE", sc.tok("sv"))
            sc.inc(v.tensor_tensor(fcol(FTMP), fcol(FM), fcol(FM), ALU.mult), "sv", 1)
            sc.wait(v, "DVE", sc.tok("sv"))
            sc.inc(v.tensor_tensor(fcol(FVAR), fcol(FVAR), fcol(FTMP), ALU.subtract),
                   "sv", 1)
        self.dve(_v2)
        var_tok = sc.tok("sv")

        def _a(a, var_tok=var_tok):
            sc.wait(a, "ACT", self.cb_tok, var_tok)
            sc.inc(a.activation(fcol(FLNV), fcol(FVAR), AF.Ln, bias=EPS), "sa", 1)
            sc.wait(a, "ACT", sc.tok("sa"))
            sc.inc(a.activation(fcol(FTMP), fcol(FLNV), AF.Exp, scale=-1.0), "sa", 1)
        self.act(_a)
        u2_tok = sc.tok("sa")

        def _v3(v, u2_tok=u2_tok):
            sc.wait(v, "DVE", u2_tok)
            sc.inc(v.tensor_tensor(fcol(FTMP), fcol(FVAR), fcol(FTMP), ALU.mult),
                   "sv", 1)
        self.dve(_v3)
        vh_tok = sc.tok("sv")

        def _a2(a, vh_tok=vh_tok):
            sc.wait(a, "ACT", vh_tok)
            sc.inc(a.activation(fcol(FTMP), fcol(FTMP), AF.Ln, bias=1.0), "sa", 1)
        self.act(_a2)
        lg_tok = sc.tok("sa")

        def _v4(v, lg_tok=lg_tok):
            sc.wait(v, "DVE", lg_tok)
            sc.inc(v.tensor_tensor(self.lacc[:, :], self.lacc[:, :], fcol(FTMP),
                                   ALU.add), "sv", 1)
        self.dve(_v4)
        self.f_pst_war = self.f_pst_war_new
        if self.f_loaded < self.f_n:
            self._feed_loadk(self.f_loaded)

    def feed_drain(self):
        while self.f_done < self.f_n:
            self.feed_step()

    def csum_and_gate(self, p_any):
        sc = self

        def _t6(t):
            sc.wait(t, "PE", sc.tok("sv"), sc.tok("sa"))
            sc.inc(t.transpose(p_any[0:1, 0:128], self.lacc[:, :], self.eye_f[:, :]),
                   "sp", 1)
        self.pe(_t6)
        la_tok = sc.tok("sp")

        def _v8(v, la_tok=la_tok):
            sc.wait(v, "DVE", la_tok)
            sc.inc(v.tensor_copy(self.f_gd[0:1, 0:128], p_any[0:1, 0:128]), "sv", 1)
            sc.wait(v, "DVE", sc.tok("sv"))
            sc.inc(v.reduce_sum(self.csum_sb[0:1, 0:1], self.f_gd[0:1, 0:128],
                                axis=AX.X), "sv", 1)
        self.dve(_v8)
        self.csum_tok = sc.tok("sv")
        if self.debug:
            def _sdbg(sq):
                sc.wait(sq, "POOL", self.csum_tok)
                sc.inc(sq.dma_start(out=self.dbg["dbg_csum"].ap()[0:1, 0:1],
                                    in_=self.csum_sb.ap()[0:1, 0:1]), "d_dbg", 16)
                sq.wait_ge(self.sem["d_dbg"], self.cnt["d_dbg"])
            self.pool(_sdbg)
        self.gate_math()

    def gate_math(self):
        sc = self
        self.row2 = self.sbx("row2", (1, NTOK), F32)
        self.row3 = self.sbx("row3", (1, NTOK), F32)

        def _v(v):
            sc.wait(v, "DVE", self.csum_tok)
            sc.inc(v.tensor_scalar(self.row2[0:1, 0:1], self.csum_sb[0:1, 0:1],
                                   1.0 / 8192.0, 1e-6, ALU.mult, ALU.add), "sv", 1)
            sc.wait(v, "DVE", sc.tok("sv"))
            sc.inc(v.reciprocal(self.row2[0:1, 1:2], self.row2[0:1, 0:1]), "sv", 1)
            sc.inc(v.tensor_scalar(self.row2[0:1, 2:3], self.scal_sb[:, 2:3], -1.0,
                                   None, ALU.mult), "sv", 1)
        self.dve(_v)
        r2_tok = sc.tok("sv")

        def _a(a, r2_tok=r2_tok):
            sc.wait(a, "ACT", r2_tok)
            sc.inc(a.activation(self.row3[0:1, :], self.learned_row[0:1, :],
                                AF.Exp, scale=-1.0, bias=self.row2[0:1, 2:3]),
                   "sa", 1)
        self.act(_a)
        sig_tok = sc.tok("sa")

        def _v2a(v, sig_tok=sig_tok):
            sc.wait(v, "DVE", sig_tok)
            sc.inc(v.tensor_scalar(self.row3[:, :], self.row3[:, :], 1.0, None,
                                   ALU.add), "sv", 1)
        self.dve(_v2a)
        d1_tok = sc.tok("sv")

        def _a1r(a, d1_tok=d1_tok):
            sc.wait(a, "ACT", d1_tok)
            sc.inc(a.activation(self.row3[:, :], self.row3[:, :], AF.Ln), "sa", 1)
            sc.wait(a, "ACT", sc.tok("sa"))
            sc.inc(a.activation(self.row3[:, :], self.row3[:, :], AF.Exp,
                                scale=-1.0), "sa", 1)
        self.act(_a1r)
        s1_tok = sc.tok("sa")

        def _v2(v, s1_tok=s1_tok):
            sc.wait(v, "DVE", s1_tok)
            sc.inc(v.tensor_scalar(self.row3[:, :], self.row3[:, :], 2.5, None,
                                   ALU.mult), "sv", 1)
            sc.inc(v.tensor_scalar(self.gate_row[:, :], self.logv_row[:, :],
                                   self.row2[0:1, 1:2], 0.5, ALU.mult, ALU.mult),
                   "sv", 1)
            sc.wait(v, "DVE", sc.tok("sv"))
            sc.inc(v.tensor_tensor(self.gate_row[:, :], self.gate_row[:, :],
                                   self.row3[:, :], ALU.add), "sv", 1)
            sc.wait(v, "DVE", sc.tok("sv"))
            sc.inc(v.tensor_scalar(self.gate_row[:, :], self.gate_row[:, :], 0.0, 5.0,
                                   ALU.max, ALU.min), "sv", 1)
            sc.wait(v, "DVE", sc.tok("sv"))
            sc.inc(v.tensor_scalar(self.gate_row[:, :], self.gate_row[:, :],
                                   1.0 / 1.5, -0.5 / 1.5, ALU.mult, ALU.add), "sv", 1)
            sc.wait(v, "DVE", sc.tok("sv"))
            sc.inc(v.tensor_scalar(self.gate_row[:, :], self.gate_row[:, :], 0.0, 1.0,
                                   ALU.max, ALU.min), "sv", 1)
            sc.wait(v, "DVE", sc.tok("sv"))
            sc.inc(v.tensor_scalar(self.gate_row[:, :], self.gate_row[:, :],
                                   self.scal_sb[:, 0:1], self.scal_sb[:, 1:2],
                                   ALU.mult, ALU.add), "sv", 1)
        self.dve(_v2)
        pre_tok = sc.tok("sv")

        def _a2(a, pre_tok=pre_tok):
            sc.wait(a, "ACT", pre_tok)
            sc.inc(a.activation(self.row3[:, :], self.gate_row[:, :], AF.Exp,
                                scale=-1.0), "sa", 1)
        self.act(_a2)
        e2_tok = sc.tok("sa")

        def _v3a(v, e2_tok=e2_tok):
            sc.wait(v, "DVE", e2_tok)
            sc.inc(v.tensor_scalar(self.row3[:, :], self.row3[:, :], 1.0, None,
                                   ALU.add), "sv", 1)
        self.dve(_v3a)
        d2_tok = sc.tok("sv")

        def _a2r(a, d2_tok=d2_tok):
            sc.wait(a, "ACT", d2_tok)
            sc.inc(a.activation(self.row3[:, :], self.row3[:, :], AF.Ln), "sa", 1)
            sc.wait(a, "ACT", sc.tok("sa"))
            sc.inc(a.activation(self.gate_row[:, :], self.row3[:, :], AF.Exp,
                                scale=-1.0), "sa", 1)
        self.act(_a2r)
        g0_tok = sc.tok("sa")

        def _v3(v, g0_tok=g0_tok):
            sc.wait(v, "DVE", g0_tok)
            sc.inc(v.tensor_scalar(self.row3[:, :], self.gate_row[:, :], 0.05, None,
                                   ALU.is_ge), "sv", 1)
            sc.wait(v, "DVE", sc.tok("sv"))
            sc.inc(v.tensor_tensor(self.gate_row[:, :], self.gate_row[:, :],
                                   self.row3[:, :], ALU.mult), "sv", 1)
        self.dve(_v3)
        self.gate_tok = sc.tok("sv")
        if self.debug:
            def _sdbg(sq):
                sc.wait(sq, "POOL", self.gate_tok)
                sc.inc(sq.dma_start(out=self.dbg["dbg_gate"].ap(),
                                    in_=self.gate_row.ap()), "d_dbg", 16)
                sq.wait_ge(self.sem["d_dbg"], self.cnt["d_dbg"])
            self.pool(_sdbg)

    def bar(self):
        toks = [self.tok(n) for n in ("sp", "sv", "sa", "sg")]
        for emit, name in [(self.pe, "PE"), (self.dve, "DVE"), (self.act, "ACT"),
                           (self.pool, "POOL"), (self.syn, "SYNC")]:
            def _f(e, name=name):
                self.wait(e, name, *toks)
            emit(_f)

    def mmacc(self, t, out_ap, ops, inc=True):
        n = len(ops)
        m = None
        for i, (l, r) in enumerate(ops):
            m = t.matmul(out_ap, l, r, start=(i == 0), stop=(i == n - 1))
        if inc:
            return self.inc(m, "sp", 1)
        return None

    def build(self):
        nc = self.nc
        din = self.din
        din("x_all_t", (4, NT, 128, 32, 128))
        din("xm_t", (128, 32, NTOK))
        din("xmT", (D, NTOK))
        din("wq_t", (32, 128, 32, 128))
        din("wo_t", (32, 128, 32, 128))
        din("wk_t", (128, 32, 128))
        din("wv_t", (128, 32, 128))
        din("wqf_t", (128, 32, BOT))
        din("wr_T", (BOT, RDIM))
        din("akT", (BOT, NSLOT))
        din("avT", (D, NSLOT))
        din("wvdT", (D, VBOT))
        din("wvuT_t", (128, 2, D))
        din("wu1T_t", (8, 128, 32, 128))
        din("wu2c", (128, 8))
        din("bu1c", (128, 8))
        din("logrel_r", (1, NSLOT))
        din("scal", (1, 4))
        din("masks", (128, 8, 2, 128), BF16)
        din("c_ones_row_b", (1, 128), BF16)
        din("c_ones_col_b", (128, 1), BF16)
        din("c_ones_row_f", (1, 128), F32)
        din("c_ones_mat_b", (128, 128), BF16)
        din("c_eye_b", (128, 128), BF16)
        din("c_eye_f", (128, 128), F32)
        din("c_sel16_f", (16, 16), F32)
        din("c_selown_f", (16, 8), F32)

        self.outT = nc.dram_tensor("outT", [D, NTOK], F32, kind="ExternalOutput")
        if self.debug:
            for name, shape in [
                ("dbg_stat", [128, 10, 16]), ("dbg_K", [128, T]),
                ("dbg_V", [128, NT, 128]), ("dbg_q", [128, 4, NTOK]),
                ("dbg_query", [BOT, NTOK]), ("dbg_rs", [128, NSLOT]),
                ("dbg_w", [128, NSLOT]), ("dbg_outv", [128, 2, NTOK]),
                ("dbg_gate", [1, NTOK]), ("dbg_att", [128, 4, NTOK]),
                ("dbg_csum", [1, 8]), ("dbg_negms", [32, NTOK]),
            ]:
                self.dbg[name] = nc.dram_tensor(name, shape, F32, kind="ExternalOutput")

        self.mksem("sp", "sv", "sa", "sg",
                   "d_cst", "d_cstg", "d_xb0", "d_xb1", "d_kv0", "d_kv1", "d_xm",
                   "d_av0", "d_av1", "d_wvd", "d_wu0",
                   "d_wq0", "d_wq1", "d_wo0", "d_wo1", "d_xf0", "d_xf1",
                   "d_out", "d_o0", "d_o1", "d_dbg")

        sb = self.sb
        self.ones_row_b = sb("ones_row_b", (1, 128), BF16)
        self.ones_col_b = sb("ones_col_b", (128, 1), BF16)
        self.ones_row_f = sb("ones_row_f", (1, 128), F32)
        self.ones_mat_b = sb("ones_mat_b", (128, 128), BF16)
        self.eye_b = sb("eye_b", (128, 128), BF16)
        self.eye_f = sb("eye_f", (128, 128), F32)
        self.sel16_f = sb("sel16_f", (16, 16), F32)
        self.selown_f = sb("selown_f", (16, 8), F32)
        self.cb = sb("cb", (128, 2), F32)
        self.K_bf = sb("K_bf", (128, T), BF16)
        self.V_bf = sb("V_bf", (128, NT, 128), BF16)
        self.u_own_b = sb("u_own_b", (128, NTOK), F32)
        self.outv_bf = sb("outv_bf", (128, 2, NTOK), BF16)
        self.wu2c_b = sb("wu2c_b", (128, 8), BF16)
        self.bu1c_f = sb("bu1c_f", (128, 8), F32)
        self.scal_sb = sb("scal_sb", (1, 4), F32)
        self.row_scr = sb("row_scr", (1, 128), F32)
        self.gate_row = sb("gate_row", (1, NTOK), F32)
        self.statc = sb("statc", (128, 10, 16), F32)
        self.lacc = sb("lacc", (128, 1), F32)
        self.learned_row = sb("learned_row", (1, NTOK), F32)
        self.logv_row = sb("logv_row", (1, NTOK), F32)
        self.stat_rows = {}
        for qn in ["m", "u", "u1", "mu", "logv"]:
            self.stat_rows[qn] = sb("str_%s" % qn, (16, 128), F32)
        self.big_a = sb("big_a", (128, 32, NTOK), BF16, side="right")

        nc.const_aps.register(F32, EPS, self.cb[:, 0:1])
        nc.const_aps.register(F32, 1.0, self.cb[:, 1:2])

        self.block = self.es.enter_context(nc.Block())

        with self.scope():
            self.p0()
        with self.scope():
            self.p_a()
        self.bar()
        with self.scope():
            self.p_own()
        self.bar()
        with self.scope():
            self.q_bf = self.sbx("q_bf", (128, 32, NTOK), BF16)
            self.feed_init()
            with self.scope():
                self.p_b()
            self.bar()
            with self.scope():
                self.p_c()
            self.feed_drain()
            self.csum_and_gate(self.p_st)
        self.bar()
        with self.scope():
            self.p_d()
        self.finish()
        self.es.close()
        return nc

    def p0(self):
        sc = self

        def _s(sq):
            for name, dst in [("c_ones_row_b", self.ones_row_b),
                              ("c_ones_col_b", self.ones_col_b),
                              ("c_ones_row_f", self.ones_row_f),
                              ("c_ones_mat_b", self.ones_mat_b),
                              ("c_eye_b", self.eye_b), ("c_eye_f", self.eye_f),
                              ("c_sel16_f", self.sel16_f), ("c_selown_f", self.selown_f),
                              ("scal", self.scal_sb), ("bu1c", self.bu1c_f)]:
                sc.inc(sq.dma_start(out=dst.ap(), in_=self.dt[name].ap()), "d_cst", 16)
        self.syn(_s)
        self.cst_tok = sc.tok("d_cst")

        def _g(gq):
            sc.inc(gq.dma_start(out=self.wu2c_b.ap(), in_=self.dt["wu2c"].ap()),
                   "d_cstg", 16)
        self.pool(_g)
        self.cstg_tok = sc.tok("d_cstg")

        def _v(v):
            v.memset(self.cb[:, 0:1], EPS)
            sc.inc(v.memset(self.cb[:, 1:2], 1.0), "sv", 1)
        self.dve(_v)
        self.cb_tok = sc.tok("sv")

        def _a(a):
            from concourse.hw_specs import get_activation_tables
            for idx, tname in enumerate(get_activation_tables(self.nc.m.arch).keys()):
                if tname == "natural_log_exp_and_others":
                    a.add_instruction(mybir.InstLoadActFuncSet(
                        name=self.nc.get_next_instruction_name(),
                        act_func_set_id=idx, ins=[], outs=[]))
                    break
        self.act(_a)


    def p_a(self):
        sc = self
        S1, S2, M_, VAR, U_, LNV, U1C, MUC, LOGV, TMPC = range(10)
        self.U_IDX = U_
        statc = self.statc

        def col(q, t):
            return statc[:, q, t:t + 1]
        self.col = col

        xkv = [self.sbx("xkv0", (128, 32, 128), BF16),
               self.sbx("xkv1", (128, 32, 128), BF16)]
        gd_scr = self.sbx("gd_scr", (128, 128), F32)
        mb_scr = self.sbx("mb_scr", (128, 128), F32)
        ub_scr = self.sbx("ub_scr", (128, 128), F32)
        mrow_scr = self.sbx("mrow_scr", (1, 128), F32)
        urow_scr = self.sbx("urow_scr", (1, 128), F32)
        lacc = self.lacc
        self.csum_sb = self.sb("csum_sb", (1, 4), F32)
        self.wkv_sb = self.sbx("wkv_sb", (128, 2, 32, 128), BF16)
        p_G = self.psx("p_G", [128, 512])
        p_Sm = self.psx("p_Sm", [128, 512])
        p_tr = self.psx("p_tr", [128, 512])
        p_MbU = self.psx("p_MbU", [128, 512])
        p_k = [self.psx("p_k0", [128, 512]), self.psx("p_k1", [128, 512])]
        p_v = [self.psx("p_v0", [128, 512]), self.psx("p_v1", [128, 512])]

        def _gw(gq):
            sc.inc(gq.dma_start(out=self.wkv_sb[:, 0], in_=self.dt["wk_t"].ap()),
                   "d_cstg", 16)
            sc.inc(gq.dma_start(out=self.wkv_sb[:, 1], in_=self.dt["wv_t"].ap()),
                   "d_cstg", 16)
        self.pool(_gw)
        self.cstg_tok = sc.tok("d_cstg")

        def _vz(v):
            sc.inc(v.memset(lacc[:, :], 0.0), "sv", 1)
        self.dve(_vz)

        kv_cons = [None] * NT
        kv_evac = [None] * NT

        def load(i):
            t_i = i
            p = i % 2

            def _g(gq):
                if i >= 2:
                    sc.wait(gq, "POOL", kv_cons[i - 2])
                sc.inc(gq.dma_start(out=xkv[p].ap(),
                                    in_=self.dt["x_all_t"].ap()[0, t_i]),
                       "d_kv%d" % p, 16)
                sc.inc(gq.dma_start(out=self.big_a.ap()[:, 2 * i:2 * i + 2, :],
                                    in_=self.dt["xm_t"].ap()[:, 2 * i:2 * i + 2, :]),
                       "d_xm", 16)
            self.pool(_g)

        load(0)
        load(1)
        pa_pst_war = None
        for i in range(NT):
            t_i = i
            own = True
            p = i % 2
            ld = sc.tok("d_kv%d" % p)

            def _t(t, i=i, p=p, ld=ld, pa_pst_war=pa_pst_war):
                sc.wait(t, "PE", ld, self.cst_tok, pa_pst_war)
                self.mmacc(t, p_G[:, 0:128],
                           [(xkv[p][:, c, :], xkv[p][:, c, :]) for c in range(32)])
                self.mmacc(t, p_Sm[:, 0:1],
                           [(xkv[p][:, c, :], self.ones_col_b[:, :])
                            for c in range(32)])
                if not own:
                    kv_cons[i] = sc.tok("sp")
            self.pe(_t)
            gm_tok = sc.tok("sp")

            def _v(v, i=i, t_i=t_i, gm_tok=gm_tok):
                sc.wait(v, "DVE", gm_tok)
                sc.inc(v.tensor_tensor(gd_scr[:, :], p_G[:, 0:128], self.eye_f[:, :],
                                       ALU.mult), "sv", 1)
                sc.wait(v, "DVE", sc.tok("sv"))
                sc.inc(v.reduce_sum(col(S2, t_i), gd_scr[:, :], axis=AX.X), "sv", 1)
                sc.inc(v.tensor_copy(col(S1, t_i), p_Sm[:, 0:1]), "sv", 1)
                sc.wait(v, "DVE", sc.tok("sv"))
                sc.inc(v.tensor_scalar(col(M_, t_i), col(S1, t_i), 1.0 / D, None,
                                       ALU.mult), "sv", 1)
                sc.inc(v.tensor_scalar(col(VAR, t_i), col(S2, t_i), 1.0 / D, None,
                                       ALU.mult), "sv", 1)
                sc.wait(v, "DVE", sc.tok("sv"))
                sc.inc(v.tensor_tensor(col(TMPC, t_i), col(M_, t_i), col(M_, t_i),
                                       ALU.mult), "sv", 1)
                sc.wait(v, "DVE", sc.tok("sv"))
                sc.inc(v.tensor_tensor(col(VAR, t_i), col(VAR, t_i), col(TMPC, t_i),
                                       ALU.subtract), "sv", 1)
            self.dve(_v)
            var_tok = sc.tok("sv")
            pa_pst_war = var_tok

            def _a(a, t_i=t_i, var_tok=var_tok):
                sc.wait(a, "ACT", self.cb_tok, var_tok)
                sc.inc(a.activation(col(LNV, t_i), col(VAR, t_i), AF.Ln, bias=EPS),
                       "sa", 1)
                sc.wait(a, "ACT", sc.tok("sa"))
                sc.inc(a.activation(col(U_, t_i), col(LNV, t_i), AF.Exp, scale=-0.5),
                       "sa", 1)
            self.act(_a)
            u_tok = sc.tok("sa")

            def _v2(v, t_i=t_i, own=own, u_tok=u_tok):
                sc.wait(v, "DVE", u_tok)
                if own:
                    sc.inc(v.tensor_scalar(col(U1C, t_i), col(U_, t_i), 1.0, None,
                                           ALU.add), "sv", 1)
                    sc.inc(v.tensor_tensor(col(MUC, t_i), col(M_, t_i), col(U_, t_i),
                                           ALU.mult), "sv", 1)
                sc.inc(v.tensor_tensor(col(TMPC, t_i), col(VAR, t_i), col(U_, t_i),
                                       ALU.mult), "sv", 1)
                sc.wait(v, "DVE", sc.tok("sv"))
                sc.inc(v.tensor_tensor(col(TMPC, t_i), col(TMPC, t_i), col(U_, t_i),
                                       ALU.mult), "sv", 1)
            self.dve(_v2)
            vh_tok = sc.tok("sv")

            def _a2(a, t_i=t_i, own=own, vh_tok=vh_tok):
                sc.wait(a, "ACT", vh_tok)
                sc.inc(a.activation(col(LOGV, t_i) if own else col(TMPC, t_i),
                                    col(TMPC, t_i), AF.Ln, bias=1.0), "sa", 1)
            self.act(_a2)
            lg_tok = sc.tok("sa")

            def _v3(v, t_i=t_i, own=own, lg_tok=lg_tok):
                sc.wait(v, "DVE", lg_tok)
                sc.inc(v.tensor_tensor(lacc[:, :], lacc[:, :],
                                       col(LOGV, t_i) if own else col(TMPC, t_i),
                                       ALU.add), "sv", 1)
            self.dve(_v3)

            if own:
                mrow_war = sc.tok("sp")

                def _t2(t, t_i=t_i, vh_tok=vh_tok):
                    sc.wait(t, "PE", vh_tok, sc.tok("sv"))
                    sc.inc(t.transpose(p_tr[0:1, 0:128],
                                       col(M_, t_i), self.eye_f[:, :]), "sp", 1)
                self.pe(_t2)
                trm_tok = sc.tok("sp")

                def _v4(v, trm_tok=trm_tok):
                    sc.wait(v, "DVE", trm_tok)
                    sc.inc(v.tensor_copy(mrow_scr[:, :], p_tr[0:1, 0:128]), "sv", 1)
                self.dve(_v4)
                mr_tok = sc.tok("sv")

                def _t3(t, t_i=t_i, mr_tok=mr_tok):
                    sc.wait(t, "PE", mr_tok)
                    sc.inc(t.matmul(p_MbU[:, 0:128], self.ones_row_f[:, :],
                                    mrow_scr[:, :], start=True, stop=True), "sp", 1)
                    sc.inc(t.transpose(p_tr[0:1, 128:256],
                                       col(U_, t_i), self.eye_f[:, :]), "sp", 1)
                self.pe(_t3)
                mbu_tok = sc.tok("sp")

                def _v5(v, p=p, mbu_tok=mbu_tok):
                    sc.wait(v, "DVE", mbu_tok)
                    sc.inc(v.tensor_copy(mb_scr[:, :], p_MbU[:, 0:128]), "sv", 1)
                    sc.inc(v.tensor_copy(urow_scr[:, :], p_tr[0:1, 128:256]), "sv", 1)
                self.dve(_v5)
                mb_tok = sc.tok("sv")

                def _t4(t, mb_tok=mb_tok):
                    sc.wait(t, "PE", mb_tok)
                    sc.inc(t.matmul(p_MbU[:, 128:256], self.ones_row_f[:, :],
                                    urow_scr[:, :], start=True, stop=True), "sp", 1)
                self.pe(_t4)
                ub_tok = sc.tok("sp")

                def _v6(v, p=p, ub_tok=ub_tok, mb_tok=mb_tok):
                    sc.wait(v, "DVE", ub_tok)
                    sc.inc(v.tensor_copy(ub_scr[:, :], p_MbU[:, 128:256]), "sv", 1)
                    m_sl = mb_scr[:, :].unsqueeze(1).broadcast_to((128, 32, 128))
                    sc.wait(v, "DVE", sc.tok("sv"))
                    sc.inc(v.tensor_tensor(xkv[p][:, :, :], xkv[p][:, :, :], m_sl,
                                           ALU.subtract), "sv", 1)
                self.dve(_v6)
                xc_tok = sc.tok("sv")

                def _t5(t, p=p, i=i, xc_tok=xc_tok):
                    sc.wait(t, "PE", self.cstg_tok, xc_tok)
                    if i >= 2:
                        sc.wait(t, "PE", kv_evac[i - 2])
                    self.mmacc(t, p_k[p][:, 0:128],
                               [(self.wkv_sb[:, 0, c, :], xkv[p][:, c, :])
                                for c in range(32)])
                    self.mmacc(t, p_v[p][:, 0:128],
                               [(xkv[p][:, c, :], self.wkv_sb[:, 1, c, :])
                                for c in range(32)])
                    kv_cons[i] = sc.tok("sp")
                self.pe(_t5)
                mm_tok = sc.tok("sp")

                def _v7(v, t_i=t_i, p=p, i=i, mm_tok=mm_tok):
                    sc.wait(v, "DVE", mm_tok)
                    sc.inc(v.tensor_tensor(self.K_bf[:, 128 * t_i:128 * (t_i + 1)],
                                           p_k[p][:, 0:128], ub_scr[:, :],
                                           ALU.mult), "sv", 1)
                    sc.inc(v.tensor_scalar(self.V_bf[:, t_i, :], p_v[p][:, 0:128],
                                           col(U_, t_i), None, ALU.mult), "sv", 1)
                    kv_evac[i] = sc.tok("sv")
                self.dve(_v7)

            if i == NT - 1:
                self._emit_stat_rows(p_tr)

            if i + 2 < NT:
                load(i + 2)

        self.kv_done = sc.tok("sv")
        if self.debug:
            def _sdbg(sq):
                sc.wait(sq, "POOL", self.kv_done, sc.tok("sa"))
                sc.inc(sq.dma_start(out=self.dbg["dbg_K"].ap(), in_=self.K_bf.ap()),
                       "d_dbg", 16)
                sc.inc(sq.dma_start(out=self.dbg["dbg_V"].ap(), in_=self.V_bf.ap()),
                       "d_dbg", 16)
                sc.inc(sq.dma_start(out=self.dbg["dbg_stat"].ap(), in_=statc.ap()),
                       "d_dbg", 16)
                sq.wait_ge(self.sem["d_dbg"], self.cnt["d_dbg"])
            self.pool(_sdbg)

    def _emit_stat_rows(self, p_tr):
        sc = self
        S1, S2, M_, VAR, U_, LNV, U1C, MUC, LOGV, TMPC = range(10)
        stat_v = sc.tok("sv")
        stat_a = sc.tok("sa")
        for qi, qn in [(M_, "m"), (U_, "u"), (U1C, "u1"), (MUC, "mu"),
                       (LOGV, "logv")]:
            def _t(t, qi=qi):
                sc.wait(t, "PE", self.cst_tok, stat_v, stat_a, sc.tok("sv"))
                sc.inc(t.transpose(p_tr[0:16, 128:256], self.statc[:, qi, :],
                                   self.eye_f[:, :]), "sp", 1)
            self.pe(_t)
            ttok = sc.tok("sp")

            def _v(v, qn=qn, ttok=ttok):
                sc.wait(v, "DVE", ttok)
                sc.inc(v.tensor_copy(self.stat_rows[qn][:, :], p_tr[0:16, 128:256]),
                       "sv", 1)
            self.dve(_v)
        self.strows_tok = sc.tok("sv")
        self.ebcast(p_tr, "u", NOWN, True, self.u_own_b)
        self.ebcast(p_tr, "logv", NOWN, True, None, dest_row=self.logv_row)
        self.bcast_tok = sc.tok("sv")

    def p_own(self):
        sc = self
        big = self.big_a
        self.akT_bf = self.sbx("akT_bf", (BOT, NSLOT), BF16)
        self.rkT_bf = self.sbx("rkT_bf", (RDIM, NSLOT), BF16)
        self.wrT_bf = self.sbx("wrT_bf", (BOT, RDIM), BF16)
        self.logrel_bf = self.sbx("logrel_bf", (1, NSLOT), BF16)
        self.query_bf = self.sbx("query_bf", (BOT, NTOK), BF16)
        self.wqf_sb = self.sbx("wqf_sb", (128, 32, BOT), BF16)
        self.m_own_b = self.sbx("m_own_b", (128, NTOK), F32)
        self.rq_bf = self.sbx("rq_bf", (RDIM, NTOK), BF16)

        def _gl(gq):
            for name, dst in [("akT", self.akT_bf), ("logrel_r", self.logrel_bf),
                              ("wr_T", self.wrT_bf), ("wqf_t", self.wqf_sb)]:
                sc.inc(gq.dma_start(out=dst.ap(), in_=self.dt[name].ap()),
                       "d_cstg", 16)
        self.pool(_gl)
        self.cstg_tok = sc.tok("d_cstg")

        with self.scope():
            p_exo = self.psx("p_exo", [128, 512])
            self.ebcast(p_exo, "m", NOWN, True, self.m_own_b)
            mo_tok = sc.tok("sv")

            for hh in range(2):
                def _trk(t, hh=hh):
                    sc.wait(t, "PE", self.cstg_tok, sc.tok("sv"))
                    sc.inc(t.matmul(p_exo[0:RDIM, 0:512], self.wrT_bf[:, :],
                                    self.akT_bf[:, 512 * hh:512 * (hh + 1)],
                                    start=True, stop=True), "sp", 1)
                self.pe(_trk)
                ptok = sc.tok("sp")

                def _vrk(v, hh=hh, ptok=ptok):
                    sc.wait(v, "DVE", ptok)
                    sc.inc(v.tensor_copy(self.rkT_bf[:, 512 * hh:512 * (hh + 1)],
                                         p_exo[0:RDIM, 0:512]), "sv", 1)
                self.dve(_vrk)

        xm_ld = sc.tok("d_xm")

        def _vsub(v):
            sc.wait(v, "DVE", xm_ld, mo_tok)
            for c in range(32):
                sc.inc(v.tensor_tensor(big[:, c, :], big[:, c, :], self.m_own_b[:, :],
                                       ALU.subtract), "sv", 1)
        self.dve(_vsub)
        self.xm_tok = sc.tok("sv")

        with self.scope():
            p_q = [self.psx("p_q0", [BOT, 512]), self.psx("p_q1", [BOT, 512])]
            for th in range(2):
                def _t(t, th=th):
                    sc.wait(t, "PE", self.cstg_tok, self.xm_tok)
                    self.mmacc(t, p_q[th][:, :],
                               [(self.wqf_sb[:, c, :], big[:, c, 512 * th:512 * (th + 1)])
                                for c in range(32)])
                self.pe(_t)
                qtok = sc.tok("sp")

                def _v(v, th=th, qtok=qtok):
                    sc.wait(v, "DVE", qtok)
                    sc.inc(v.scalar_tensor_tensor(
                        self.query_bf[:, 512 * th:512 * (th + 1)], p_q[th][:, :], 0.125,
                        self.u_own_b[0:BOT, 512 * th:512 * (th + 1)], ALU.mult,
                        ALU.mult), "sv", 1)
                self.dve(_v)
            query_tok = sc.tok("sv")

            pass
            p_rq = self.psx("p_rq", [RDIM, 512])
            for th in range(2):
                def _t(t, th=th, query_tok=query_tok):
                    sc.wait(t, "PE", query_tok, sc.tok("sv"))
                    sc.inc(t.matmul(p_rq[:, :], self.wrT_bf[:, :],
                                    self.query_bf[:, 512 * th:512 * (th + 1)],
                                    start=True, stop=True), "sp", 1)
                self.pe(_t)
                rqt = sc.tok("sp")

                def _v(v, th=th, rqt=rqt):
                    sc.wait(v, "DVE", rqt)
                    sc.inc(v.tensor_scalar(self.rq_bf[:, 512 * th:512 * (th + 1)],
                                           p_rq[:, :], 8.0 / float(48.0 ** 0.5), None,
                                           ALU.mult), "sv", 1)
                self.dve(_v)
            self.rq_tok = sc.tok("sv")
        query_tok = sc.tok("sv")

        self.bar()
        self.AVd_bf = self.sbx("AVd_bf", (128, 8, VBOT), BF16)
        with self.scope():
            avb = [self.sbx("avb0", (128, NSLOT), BF16),
                   self.sbx("avb1", (128, NSLOT), BF16)]
            wvdb = [self.sbx("wvdb0", (128, VBOT), BF16),
                    self.sbx("wvdb1", (128, VBOT), BF16)]
            p_avd = [self.psx("p_avd%d" % i, [128, 512]) for i in range(8)]
            av_cons = [None] * 32

            def avload(hc):
                p = hc % 2

                def _g(gq, hc=hc, p=p):
                    if hc >= 2:
                        sc.wait(gq, "POOL", av_cons[hc - 2])
                    sc.inc(gq.dma_start(
                        out=avb[p].ap(),
                        in_=self.dt["avT"].ap()[128 * hc:128 * (hc + 1), :]),
                        "d_av%d" % p, 16)
                    sc.inc(gq.dma_start(
                        out=wvdb[p].ap(),
                        in_=self.dt["wvdT"].ap()[128 * hc:128 * (hc + 1), :]),
                        "d_av%d" % p, 16)
                self.pool(_g)

            avload(0)
            avload(1)
            for hc in range(32):
                p = hc % 2
                ld = sc.tok("d_av%d" % p)

                def _t(t, hc=hc, p=p, ld=ld):
                    sc.wait(t, "PE", ld)
                    m = None
                    for nch in range(8):
                        m = t.matmul(p_avd[nch][:, 0:VBOT],
                                     avb[p][:, 128 * nch:128 * (nch + 1)],
                                     wvdb[p][:, :], start=(hc == 0), stop=(hc == 31))
                    sc.inc(m, "sp", 1)
                    av_cons[hc] = sc.tok("sp")
                self.pe(_t)
                if hc + 2 < 32:
                    avload(hc + 2)
            avd_mm = sc.tok("sp")

            def _v(v, avd_mm=avd_mm):
                sc.wait(v, "DVE", avd_mm)
                for nch in range(8):
                    sc.inc(v.tensor_copy(self.AVd_bf[:, nch, :], p_avd[nch][:, 0:VBOT]),
                           "sv", 1)
            self.dve(_v)
        avd_tok = sc.tok("sv")

        self.bar()
        with self.scope():
            rs_bf = self.sbx("rs_bf", (128, NSLOT), BF16)
            e_f = self.sbx("e_f", (128, NSLOT), F32)
            wm_bf = self.sbx("wm_bf", (128, NSLOT), BF16)
            w_bf = self.sbx("w_bf", (128, NSLOT), BF16)
            wT_bf = self.sbx("wT_bf", (128, 8, 128), BF16)
            top8 = self.sbx("top8", (128, 8), F32)
            acol = self.sbx("acol", (128, 4), F32)
            p_A = self.psx("p_A", [128, 1024])
            p_RS = self.psx("p_RS", [128, 1024])
            p_wT = self.psx("p_wT", [128, 512], BF16)
            p_ov = [self.psx("p_ov0", [128, 512]), self.psx("p_ov1", [128, 512])]

            prev_pa_v = None
            prev_pa_a = None
            for j in range(NOWN):
                def _t(t, j=j, prev_pa_v=prev_pa_v, prev_pa_a=prev_pa_a):
                    sc.wait(t, "PE", query_tok, self.rq_tok, self.cstg_tok,
                            prev_pa_v, prev_pa_a)
                    for nh in range(2):
                        t.matmul(p_A[:, 512 * nh:512 * (nh + 1)],
                                 self.query_bf[:, 128 * j:128 * (j + 1)],
                                 self.akT_bf[:, 512 * nh:512 * (nh + 1)],
                                 start=True, stop=False)
                        t.matmul(p_A[:, 512 * nh:512 * (nh + 1)], self.ones_row_b[:, :],
                                 self.logrel_bf[:, 512 * nh:512 * (nh + 1)],
                                 start=False, stop=True)
                    m = None
                    for nh in range(2):
                        t.matmul(p_RS[:, 512 * nh:512 * (nh + 1)],
                                 self.rq_bf[:, 128 * j:128 * (j + 1)],
                                 self.rkT_bf[:, 512 * nh:512 * (nh + 1)],
                                 start=True, stop=False)
                        m = t.matmul(p_RS[:, 512 * nh:512 * (nh + 1)],
                                     self.ones_row_b[:, :],
                                     self.logrel_bf[:, 512 * nh:512 * (nh + 1)],
                                     start=False, stop=True)
                    sc.inc(m, "sp", 1)
                self.pe(_t)
                mm_tok = sc.tok("sp")

                def _v(v, j=j, mm_tok=mm_tok):
                    sc.wait(v, "DVE", mm_tok)
                    sc.inc(v.tensor_copy(rs_bf[:, :], p_RS[:, :]), "sv", 1)
                    sc.inc(v.reduce_max(acol[:, 0:1], p_A[:, :], axis=AX.X,
                                        negate=True), "sv", 1)
                    sc.wait(v, "DVE", sc.tok("sv"))
                    sc.inc(v.max(top8[:, :], rs_bf[:, :]), "sv", 1)
                self.dve(_v)
                vtok = sc.tok("sv")
                prev_pa_v = vtok

                def _a(a, j=j, vtok=vtok):
                    sc.wait(a, "ACT", vtok)
                    sc.inc(a.activation(e_f[:, :], p_A[:, :], AF.Exp,
                                        bias=acol[:, 0:1]), "sa", 1)
                self.act(_a)
                atok = sc.tok("sa")
                prev_pa_a = atok

                def _v2(v, j=j, atok=atok):
                    sc.wait(v, "DVE", atok)
                    sc.inc(v.scalar_tensor_tensor(wm_bf[:, :], rs_bf[:, :],
                                                  top8[:, 7:8], e_f[:, :], ALU.is_ge,
                                                  ALU.mult, accum_out=acol[:, 1:2]),
                           "sv", 1)
                    sc.wait(v, "DVE", sc.tok("sv"))
                    sc.inc(v.reciprocal(acol[:, 2:3], acol[:, 1:2]), "sv", 1)
                    sc.wait(v, "DVE", sc.tok("sv"))
                    sc.inc(v.tensor_scalar(w_bf[:, :], wm_bf[:, :], acol[:, 2:3], None,
                                           ALU.mult), "sv", 1)
                self.dve(_v2)
                w_tok = sc.tok("sv")
                if self.debug and j == NOWN - 1:
                    def _sdbg(sq, w_tok=w_tok):
                        sc.wait(sq, "POOL", w_tok)
                        sc.inc(sq.dma_start(out=self.dbg["dbg_rs"].ap(),
                                            in_=rs_bf.ap()), "d_dbg", 16)
                        sc.inc(sq.dma_start(out=self.dbg["dbg_w"].ap(),
                                            in_=w_bf.ap()), "d_dbg", 16)
                        sq.wait_ge(self.sem["d_dbg"], self.cnt["d_dbg"])
                    self.pool(_sdbg)

                for nch in range(8):
                    def _t2(t, nch=nch, w_tok=w_tok):
                        sc.wait(t, "PE", w_tok, sc.tok("sv"))
                        sc.inc(t.transpose(p_wT[:, 0:128],
                                           w_bf[:, 128 * nch:128 * (nch + 1)],
                                           self.eye_b[:, :]), "sp", 1)
                    self.pe(_t2)
                    ttok = sc.tok("sp")

                    def _v3(v, nch=nch, ttok=ttok):
                        sc.wait(v, "DVE", ttok)
                        sc.inc(v.tensor_copy(wT_bf[:, nch, :], p_wT[:, 0:128]), "sv", 1)
                    self.dve(_v3)
                wT_tok = sc.tok("sv")

                def _t3(t, j=j, wT_tok=wT_tok):
                    sc.wait(t, "PE", wT_tok, avd_tok)
                    for vd in range(2):
                        self.mmacc(t, p_ov[vd][:, 0:128],
                                   [(self.AVd_bf[:, nch, 128 * vd:128 * (vd + 1)],
                                     wT_bf[:, nch, :]) for nch in range(8)])
                self.pe(_t3)
                ov_tok = sc.tok("sp")

                def _v4(v, j=j, ov_tok=ov_tok):
                    sc.wait(v, "DVE", ov_tok)
                    for vd in range(2):
                        sc.inc(v.tensor_copy(self.outv_bf[:, vd, 128 * j:128 * (j + 1)],
                                             p_ov[vd][:, 0:128]), "sv", 1)
                self.dve(_v4)
        self.outv_tok = sc.tok("sv")

        if self.debug:
            def _sdbg(sq):
                sc.wait(sq, "POOL", self.outv_tok, sc.tok("sv"))
                sc.inc(sq.dma_start(out=self.dbg["dbg_outv"].ap(),
                                    in_=self.outv_bf.ap()), "d_dbg", 16)
                sc.inc(sq.dma_start(out=self.dbg["dbg_query"].ap(),
                                    in_=self.query_bf.ap()), "d_dbg", 16)
                sq.wait_ge(self.sem["d_dbg"], self.cnt["d_dbg"])
            self.pool(_sdbg)

    def wu1_init(self):
        sc = self
        self.p_lr = self.psx("p_lr0", [1, 512])
        self.wu_wub = self.sbx("wub", (128, 32, 128), BF16)
        self.wu_rf = self.sbx("rf_f", (128, NTOK), F32)
        self.wu_tb = self.sbx("tb_bf", (128, NTOK), BF16)
        self.wu_z = self.sbx("z_bf", (128, NTOK), BF16)
        self.wu_g = self.sbx("g_bf", (128, NTOK), BF16)
        self.wu_pl = [self.psx("p_l0", [128, 512]), self.psx("p_l1", [128, 512])]
        self.wu_cons = [None] * 8
        self.wu_pl_war = None
        self.wu_lr_war = None
        self.wu_pending = None
        self.wu1_done = 0

        def wuload(uc):
            def _g(gq, uc=uc):
                if uc >= 1:
                    sc.wait(gq, "POOL", self.wu_cons[uc - 1])
                sc.inc(gq.dma_start(out=self.wu_wub.ap(),
                                    in_=self.dt["wu1T_t"].ap()[uc]), "d_wu0", 16)
            self.pool(_g)
        self._wuload = wuload
        wuload(0)

    def wu1_step(self):
        sc = self
        big = self.big_a
        uc = self.wu1_done
        self.wu1_done += 1
        wub, rf, tb, z_bf, g_bf, p_l = (self.wu_wub, self.wu_rf, self.wu_tb,
                                        self.wu_z, self.wu_g, self.wu_pl)
        if self.wu_pending is not None:
            self._wu_learned(*self.wu_pending)
            self.wu_pending = None
        ld = sc.tok("d_wu0")

        def _t(t, uc=uc, ld=ld, plwar=self.wu_pl_war):
            sc.wait(t, "PE", ld, self.xm_tok, plwar)
            for th in range(2):
                self.mmacc(t, p_l[th][:, :],
                           [(wub[:, c, :], big[:, c, 512 * th:512 * (th + 1)])
                            for c in range(32)])
            self.wu_cons[uc] = sc.tok("sp")
        self.pe(_t)
        l_tok = sc.tok("sp")
        if uc + 1 < 8:
            self._wuload(uc + 1)

        def _v(v, uc=uc, l_tok=l_tok):
            sc.wait(v, "DVE", l_tok)
            for th in range(2):
                sc.inc(v.tensor_tensor(rf[:, 512 * th:512 * (th + 1)],
                                       p_l[th][:, :],
                                       self.u_own_b[:, 512 * th:512 * (th + 1)],
                                       ALU.mult), "sv", 1)
            sc.wait(v, "DVE", sc.tok("sv"))
            sc.inc(v.tensor_scalar(z_bf[:, :], rf[:, :],
                                   self.bu1c_f[:, uc:uc + 1], None, ALU.add),
                   "sv", 1)
            self.wu_pl_war = sc.tok("sv")
            sc.wait(v, "DVE", sc.tok("sv"))
            sc.inc(v.tensor_tensor(tb[:, :], z_bf[:, :], z_bf[:, :], ALU.mult),
                   "sv", 1)
            sc.wait(v, "DVE", sc.tok("sv"))
            sc.inc(v.tensor_scalar(tb[:, :], tb[:, :], 2.0 * GELU_C2,
                                   2.0 * GELU_C1, ALU.mult, ALU.add), "sv", 1)
            sc.wait(v, "DVE", sc.tok("sv"))
            sc.inc(v.tensor_tensor(tb[:, :], z_bf[:, :], tb[:, :], ALU.mult),
                   "sv", 1)
        self.dve(_v)
        z2_tok = sc.tok("sv")

        def _a(a, uc=uc, z2_tok=z2_tok):
            sc.wait(a, "ACT", z2_tok)
            sc.inc(a.activation(rf[:, :], tb[:, :], AF.Exp, scale=-1.0), "sa", 1)
        self.act(_a)
        e_tok = sc.tok("sa")

        def _v2(v, uc=uc, e_tok=e_tok):
            sc.wait(v, "DVE", e_tok)
            sc.inc(v.tensor_scalar(rf[:, :], rf[:, :], 1.0, None, ALU.add), "sv", 1)
        self.dve(_v2)
        d1p_tok = sc.tok("sv")

        def _a2(a, uc=uc, d1p_tok=d1p_tok):
            sc.wait(a, "ACT", d1p_tok)
            sc.inc(a.activation(rf[:, :], rf[:, :], AF.Ln), "sa", 1)
            sc.wait(a, "ACT", sc.tok("sa"))
            sc.inc(a.activation(rf[:, :], rf[:, :], AF.Exp, scale=-1.0), "sa", 1)
        self.act(_a2)
        sig_tok = sc.tok("sa")

        def _v3(v, uc=uc, sig_tok=sig_tok):
            sc.wait(v, "DVE", sig_tok)
            sc.inc(v.tensor_tensor(g_bf[:, :], z_bf[:, :], rf[:, :], ALU.mult),
                   "sv", 1)
        self.dve(_v3)
        g_tok = sc.tok("sv")
        self.wu_pending = (uc, g_tok)

    def _wu_learned(self, uc, g_tok):
        sc = self
        g_bf = self.wu_g
        for th in range(2):
            def _t2(t, uc=uc, th=th, g_tok=g_tok, lr_war=self.wu_lr_war):
                sc.wait(t, "PE", g_tok, lr_war)
                sc.inc(t.matmul(self.p_lr[:, :], self.wu2c_b[:, uc:uc + 1],
                                g_bf[:, 512 * th:512 * (th + 1)],
                                start=True, stop=True), "sp", 1)
            self.pe(_t2)
            lr_tok = sc.tok("sp")

            def _vlr(v, uc=uc, th=th, lr_tok=lr_tok):
                sc.wait(v, "DVE", lr_tok)
                if uc == 0:
                    sc.inc(v.tensor_copy(
                        self.learned_row[0:1, 512 * th:512 * (th + 1)],
                        self.p_lr[0:1, :]), "sv", 1)
                else:
                    sc.inc(v.tensor_tensor(
                        self.learned_row[0:1, 512 * th:512 * (th + 1)],
                        self.learned_row[0:1, 512 * th:512 * (th + 1)],
                        self.p_lr[0:1, :], ALU.add), "sv", 1)
            self.dve(_vlr)
            self.wu_lr_war = sc.tok("sv")

    def wu1_finish(self):
        if self.wu_pending is not None:
            self._wu_learned(*self.wu_pending)
            self.wu_pending = None
        self.learned_tok = self.tok("sv")

    def p_b(self):
        sc = self
        big = self.big_a
        wqb = [self.sbx("wqb0", (128, 16, 128), BF16),
               self.sbx("wqb1", (128, 16, 128), BF16)]
        p_qp = [self.psx("p_qp%d" % i, [128, 512]) for i in range(4)]
        wq_cons = [None] * 64
        wq_evac = [None] * 32
        self.wu1_init()

        def load(hb):
            blk, hf = divmod(hb, 2)
            p = hb % 2

            def _g(gq, hb=hb, blk=blk, hf=hf, p=p):
                if hb >= 2:
                    sc.wait(gq, "POOL", wq_cons[hb - 2])
                sc.inc(gq.dma_start(
                    out=wqb[p].ap(),
                    in_=self.dt["wq_t"].ap()[blk][:, 16 * hf:16 * (hf + 1), :]),
                    "d_wq%d" % p, 16)
            self.pool(_g)

        load(0)
        load(1)
        for blk in range(32):
            def _t(t, blk=blk):
                bp = blk % 2
                sc.wait(t, "PE", self.xm_tok)
                if blk >= 2:
                    sc.wait(t, "PE", wq_evac[blk - 2])
                for hf in range(2):
                    pb = (2 * blk + hf) % 2
                    sc.wait(t, "PE", ("d_wq%d" % pb, 16 * (blk + 1)))
                    for ci in range(16):
                        c = 16 * hf + ci
                        for th in range(2):
                            m = t.matmul(
                                p_qp[2 * bp + th][:, :], wqb[pb][:, ci, :],
                                big[:, c, 512 * th:512 * (th + 1)],
                                start=(c == 0), stop=(c == 31))
                    sc.inc(m, "sp", 1)
                    wq_cons[2 * blk + hf] = sc.tok("sp")
            self.pe(_t)
            mm_tok = sc.tok("sp")

            def _v(v, blk=blk, mm_tok=mm_tok):
                bp = blk % 2
                sc.wait(v, "DVE", mm_tok)
                for th in range(2):
                    sc.inc(v.tensor_tensor(
                        self.q_bf[:, blk, 512 * th:512 * (th + 1)],
                        p_qp[2 * bp + th][:, :],
                        self.u_own_b[:, 512 * th:512 * (th + 1)], ALU.mult), "sv", 1)
                wq_evac[blk] = sc.tok("sv")
            self.dve(_v)
            for hb in (2 * blk + 2, 2 * blk + 3):
                if hb < 64:
                    load(hb)
            if blk % 2 == 0:
                self.feed_step()
            if blk % 4 == 3:
                self.wu1_step()
        self.q_tok = sc.tok("sv")
        while self.wu1_done < 8:
            self.wu1_step()
        self.wu1_finish()
        if self.debug:
            def _sdbg(sq):
                sc.wait(sq, "POOL", self.q_tok)
                sc.inc(sq.dma_start(out=self.dbg["dbg_q"].ap(),
                                    in_=self.q_bf.ap()[:, 0:4, :]), "d_dbg", 16)
                sq.wait_ge(self.sem["d_dbg"], self.cnt["d_dbg"])
            self.pool(_sdbg)

    def p_c(self):
        sc = self
        big = self.big_a
        with self.scope():
            self.masks_sb = self.sbx("masks_sb", (128, 8, 2, 128), BF16)

            def _sld(sq):
                sc.inc(sq.dma_start(out=self.masks_sb.ap(),
                                    in_=self.dt["masks"].ap()), "d_cst", 16)
            self.syn(_sld)
            self.cst_tok = sc.tok("d_cst")
            negmg_col = self.sbx("negmg_col", (128, 1), F32)
            q2_bf = self.sbx("q2_bf", (128, NTOK), BF16)
            kcol = self.sbx("kcol", (1, 4), F32)
            with self.scope():
                k2_bf = self.sbx("k2_bf", (128, T), BF16)
                krow = self.sbx("krow", (1, 512), F32)
                kmx4 = self.sbx("kmx4", (1, 4), F32)
                p_k2 = self.psx("p_k2", [1, 512])

                def _v(v):
                    sc.wait(v, "DVE", self.kv_done, sc.tok("sp"))
                    sc.inc(v.tensor_tensor(k2_bf[:, :], self.K_bf[:, :],
                                           self.K_bf[:, :], ALU.mult), "sv", 1)
                self.dve(_v)
                k2_tok = sc.tok("sv")

                for kh in range(4):
                    def _t(t, kh=kh, k2_tok=k2_tok):
                        sc.wait(t, "PE", k2_tok, sc.tok("sv"))
                        sc.inc(t.matmul(p_k2[:, :], self.ones_col_b[:, :],
                                        k2_bf[:, 512 * kh:512 * (kh + 1)],
                                        start=True, stop=True), "sp", 1)
                    self.pe(_t)
                    ktok = sc.tok("sp")

                    def _v1(v, kh=kh, ktok=ktok):
                        sc.wait(v, "DVE", ktok)
                        sc.inc(v.tensor_copy(krow[0:1, :], p_k2[:, :]), "sv", 1)
                        sc.wait(v, "DVE", sc.tok("sv"))
                        sc.inc(v.reduce_max(kmx4[0:1, kh:kh + 1], krow[0:1, :],
                                            axis=AX.X), "sv", 1)
                    self.dve(_v1)

                def _v2(v):
                    sc.wait(v, "DVE", sc.tok("sv"))
                    sc.inc(v.reduce_max(kcol[0:1, 0:1], kmx4[0:1, :], axis=AX.X),
                           "sv", 1)
                self.dve(_v2)
                km_tok = sc.tok("sv")

                def _a(a, km_tok=km_tok):
                    sc.wait(a, "ACT", km_tok)
                    sc.inc(a.activation(kcol[0:1, 1:2], kcol[0:1, 0:1], AF.Ln),
                           "sa", 1)
                    sc.wait(a, "ACT", sc.tok("sa"))
                    sc.inc(a.activation(kcol[0:1, 2:3], kcol[0:1, 1:2], AF.Exp,
                                        scale=0.5), "sa", 1)
                self.act(_a)
                km2_tok = sc.tok("sa")

                def _v3(v, km2_tok=km2_tok):
                    sc.wait(v, "DVE", km2_tok)
                    sc.inc(v.tensor_scalar(kcol[0:1, 3:4], kcol[0:1, 2:3], -1.05,
                                           None, ALU.mult), "sv", 1)
                self.dve(_v3)
                kneg_tok = sc.tok("sv")

            kneg_tok2 = sc.tok("sv")

            PT_bf = self.sbx("PT_bf", (128, 2560), BF16)
            rsb = self.sbx("rsb", (128, 512), F32)
            rscr = self.sbx("rscr", (128, 512), F32)
            p_pair = [self.psx("p_pr%d" % i, [128, 1024]) for i in range(2)]
            p_av = self.psx("p_av", [128, 512])
            p_sum = self.psx("p_sum", [128, 512])
            p_nm = self.psx("p_nm", [128, 512])

            def chunks_for(qh):
                base = 512 * qh
                out = []
                for c in range(16):
                    qo = max(base, 128 * (c // 2))
                    wdt = base + 512 - qo
                    if wdt <= 0 or wdt > 512:
                        wdt = min(wdt, 512)
                    if wdt <= 0:
                        continue
                    out.append((c, qo, wdt))
                return out

            av_evac = None
            pend = None
            pair_evac = [None, None]
            prev_ntok = None
            prev_nmread = None
            PT2 = [PT_bf, self.sbx("PT_bf2", (128, 6656), BF16)]

            def emit_av(st):
                h, qh, chs, off, exp_done, PTb = st
                base = 512 * qh

                def _t2(t, av_war=av_evac):
                    sc.wait(t, "PE", exp_done, av_war)
                    n = len(chs)
                    for (i, (c, qo, wdt)) in enumerate(chs):
                        t.matmul(p_av[:, qo - base:512], self.V_bf[:, c, :],
                                 PTb[:, off[c]:off[c] + wdt],
                                 start=(i == 0), stop=(i == n - 1))
                    m = None
                    for (i, (c, qo, wdt)) in enumerate(chs):
                        m = t.matmul(p_sum[:, qo - base:512],
                                     self.ones_mat_b[:, :],
                                     PTb[:, off[c]:off[c] + wdt],
                                     start=(i == 0), stop=(i == n - 1))
                    sc.inc(m, "sp", 1)
                self.pe(_t2)
                av_tok = sc.tok("sp")

                def _a6(a, av_tok=av_tok):
                    sc.wait(a, "ACT", av_tok, sc.tok("sa"))
                    sc.inc(a.activation(rscr[:, :], p_sum[:, :], AF.Ln), "sa", 1)
                    sc.wait(a, "ACT", sc.tok("sa"))
                    sc.inc(a.activation(rsb[:, :], rscr[:, :], AF.Exp, scale=-1.0),
                           "sa", 1)
                self.act(_a6)

                def _v7(v, h=h, qh=qh, rtok=sc.tok("sa")):
                    sc.wait(v, "DVE", rtok)
                    sc.inc(v.tensor_tensor(big[:, h, 512 * qh:512 * (qh + 1)],
                                           p_av[:, :], rsb[:, :], ALU.mult),
                           "sv", 1)
                self.dve(_v7)
                self.feed_step()
                return sc.tok("sv")

            for h in range(H):
                def _vq2(v, h=h, prev_ntok=prev_ntok):
                    sc.wait(v, "DVE", self.q_tok, prev_ntok)
                    sc.inc(v.tensor_tensor(q2_bf[:, :], self.q_bf[:, h, :],
                                           self.q_bf[:, h, :], ALU.mult), "sv", 1)
                self.dve(_vq2)
                q2_tok = sc.tok("sv")

                nm_war = prev_nmread
                for th2 in range(2):
                    def _tq2(t, h=h, th2=th2, q2_tok=q2_tok, nm_war=nm_war):
                        sc.wait(t, "PE", q2_tok, nm_war)
                        sc.inc(t.matmul(p_nm[:, :], self.ones_mat_b[:, :],
                                        q2_bf[:, 512 * th2:512 * (th2 + 1)],
                                        start=True, stop=True), "sp", 1)
                    self.pe(_tq2)
                    ntok = sc.tok("sp")

                    def _vq3(v, th2=th2, ntok=ntok):
                        sc.wait(v, "DVE", ntok)
                        sc.inc(v.reduce_max(kcol[0:1, th2:th2 + 1], p_nm[0:1, :],
                                            axis=AX.X), "sv", 1)
                    self.dve(_vq3)
                    nm_war = sc.tok("sv")

                prev_ntok = sc.tok("sp")

                def _vq4(v):
                    sc.wait(v, "DVE", sc.tok("sv"))
                    sc.inc(v.tensor_tensor(kcol[0:1, 1:2], kcol[0:1, 0:1],
                                           kcol[0:1, 1:2], ALU.max), "sv", 1)
                self.dve(_vq4)
                mx_tok = sc.tok("sv")

                def _aq3(a, mx_tok=mx_tok):
                    sc.wait(a, "ACT", mx_tok, kneg_tok2)
                    sc.inc(a.activation(kcol[0:1, 0:1], kcol[0:1, 1:2], AF.Ln),
                           "sa", 1)
                    sc.wait(a, "ACT", sc.tok("sa"))
                    sc.inc(a.activation(kcol[0:1, 1:2], kcol[0:1, 0:1], AF.Exp,
                                        scale=0.5), "sa", 1)
                self.act(_aq3)
                mag_tok = sc.tok("sa")

                def _vnm(v, mag_tok=mag_tok):
                    sc.wait(v, "DVE", mag_tok)
                    sc.inc(v.tensor_scalar(kcol[0:1, 2:3], kcol[0:1, 1:2],
                                           kcol[0:1, 3:4], -SQRT_DH,
                                           ALU.mult, ALU.add), "sv", 1)
                    sc.wait(v, "DVE", sc.tok("sv"))
                    sc.inc(v.tensor_scalar(kcol[0:1, 2:3], kcol[0:1, 2:3], S_ATT,
                                           None, ALU.mult), "sv", 1)
                self.dve(_vnm)
                sct_tok = sc.tok("sv")

                def _tnb(t, sct_tok=sct_tok):
                    sc.wait(t, "PE", sct_tok)
                    sc.inc(t.matmul(p_nm[:, 0:1], self.ones_row_f[:, :],
                                    kcol[0:1, 2:3], start=True, stop=True), "sp", 1)
                self.pe(_tnb)
                nb_tok = sc.tok("sp")

                def _vnb(v, nb_tok=nb_tok, pe1=pair_evac[0], pe2=pair_evac[1]):
                    sc.wait(v, "DVE", nb_tok, pe1, pe2)
                    sc.inc(v.tensor_copy(negmg_col[:, :], p_nm[:, 0:1]), "sv", 1)
                self.dve(_vnb)
                nm_tok = sc.tok("sv")
                prev_nmread = nm_tok

                for qh in range(2):
                    PTb = PT2[qh]
                    base = 512 * qh
                    chs = chunks_for(qh)
                    off = {}
                    run = 0
                    for (c, qo, wdt) in chs:
                        off[c] = run
                        run += wdt
                    for pi in range(0, len(chs), 2):
                        par = (pi // 2) % 2
                        pair = chs[pi:pi + 2]

                        def _t1(t, h=h, qh=qh, pair=pair, par=par, nm_tok=nm_tok,
                                pe_war=pair_evac[par]):
                            sc.wait(t, "PE", self.q_tok, self.kv_done, nm_tok,
                                    self.cst_tok, pe_war)
                            for (ci, (c, qo, wdt)) in enumerate(pair):
                                psl = p_pair[par][:, 512 * ci:512 * ci + wdt]
                                js = c // 2
                                has_mask = (128 * js >= base)
                                m = t.matmul(psl,
                                             self.K_bf[:, 128 * c:128 * (c + 1)],
                                             self.q_bf[:, h, qo:qo + wdt],
                                             start=True, stop=(not has_mask))
                                if has_mask:
                                    m = t.matmul(
                                        p_pair[par][:, 512 * ci:512 * ci + 128],
                                        self.eye_b[:, :],
                                        self.masks_sb[:, js, c % 2, :],
                                        start=False, stop=True)
                            sc.inc(m, "sp", 1)
                        self.pe(_t1)
                        st_tok = sc.tok("sp")

                        def _a1(a, pair=pair, par=par, st_tok=st_tok, off=off,
                                PTb=PTb):
                            sc.wait(a, "ACT", st_tok)
                            (c0, qo0, w0) = pair[0]
                            if len(pair) == 2:
                                srcv = p_pair[par][:, 0:1024]                                     .rearrange("p (i w) -> p i w", i=2)[:, :, 0:w0]
                                dst = PTb[:, off[c0]:off[c0] + 2 * w0]                                     .rearrange("p (i w) -> p i w", i=2)
                                sc.inc(a.activation(dst, srcv, AF.Exp, scale=S_ATT,
                                                    bias=negmg_col[:, :]), "sa", 1)
                            else:
                                sc.inc(a.activation(PTb[:, off[c0]:off[c0] + w0],
                                                    p_pair[par][:, 0:w0], AF.Exp,
                                                    scale=S_ATT,
                                                    bias=negmg_col[:, :]), "sa", 1)
                        self.act(_a1)
                        pair_evac[par] = sc.tok("sa")

                    st = (h, qh, chs, off, sc.tok("sa"), PTb)
                    if pend is not None:
                        av_evac = emit_av(pend)
                    pend = st
            av_evac = emit_av(pend)
        self.att_tok = sc.tok("sv")
        if self.debug:
            def _sdbg(sq):
                sc.wait(sq, "POOL", self.att_tok)
                sc.inc(sq.dma_start(out=self.dbg["dbg_att"].ap(),
                                    in_=big.ap()[:, 0:4, :]), "d_dbg", 16)
                sq.wait_ge(self.sem["d_dbg"], self.cnt["d_dbg"])
            self.pool(_sdbg)

    def p_d(self):
        sc = self
        big = self.big_a
        self.u1_b = self.sbx("u1_b", (128, NTOK), F32)
        self.mu_b = self.sbx("mu_b", (128, NTOK), F32)
        p_exd = self.psx("p_exd", [128, 512])
        self.ebcast(p_exd, "u1", NOWN, True, self.u1_b)
        self.ebcast(p_exd, "mu", NOWN, True, self.mu_b)
        umu_tok = sc.tok("sv")
        wvuT_sb = self.sbx("wvuT_sb", (128, 2, D), BF16)
        gate_b = self.sbx("gate_b", (128, NTOK), F32)
        wob = [self.sbx("wob0", (128, 32, 128), BF16),
               self.sbx("wob1", (128, 32, 128), BF16)]
        xfb = [self.sbx("xfb0", (128, NTOK), F32),
               self.sbx("xfb1", (128, NTOK), F32)]
        auxg = self.sbx("auxg", (128, NTOK), F32)
        outst = [self.sbx("outst0", (128, NTOK), F32),
                 self.sbx("outst1", (128, NTOK), F32)]
        p_wo = [self.psx("p_wo%d" % i, [128, 512]) for i in range(4)]
        p_ax = [self.psx("p_ax%d" % i, [128, 512]) for i in range(2)]
        p_gb = self.psx("p_gb", [128, 512])

        def _g(gq):
            sc.inc(gq.dma_start(out=wvuT_sb.ap(), in_=self.dt["wvuT_t"].ap()),
                   "d_wvd", 16)
        self.pool(_g)
        wvu_tok = sc.tok("d_wvd")

        for th in range(2):
            def _t(t, th=th):
                sc.wait(t, "PE", self.gate_tok, sc.tok("sv"))
                sc.inc(t.matmul(p_gb[:, :], self.ones_row_f[:, :],
                                self.gate_row[0:1, 512 * th:512 * (th + 1)],
                                start=True, stop=True), "sp", 1)
            self.pe(_t)
            gtok = sc.tok("sp")

            def _v(v, th=th, gtok=gtok):
                sc.wait(v, "DVE", gtok)
                sc.inc(v.tensor_copy(gate_b[:, 512 * th:512 * (th + 1)], p_gb[:, :]),
                       "sv", 1)
            self.dve(_v)
        gb_tok = sc.tok("sv")

        wo_cons = [None] * 32
        out_done = [None] * 32
        xf_cons = [None] * 32
        ax_done = [None] * 32
        st_sent = [None] * 32

        def load(dc):
            p = dc % 2

            def _g(gq, dc=dc, p=p):
                if dc >= 2:
                    sc.wait(gq, "POOL", wo_cons[dc - 2])
                sc.inc(gq.dma_start(out=wob[p].ap(), in_=self.dt["wo_t"].ap()[dc]),
                       "d_wo%d" % p, 16)
            self.pool(_g)

            def _s(sq, dc=dc, p=p):
                if dc >= 2:
                    sc.wait(sq, "SYNC", xf_cons[dc - 2])
                sc.inc(sq.dma_start(out=xfb[p].ap(),
                                    in_=self.dt["xmT"].ap()[128 * dc:128 * (dc + 1), :]),
                       "d_xf%d" % p, 16)
            self.syn(_s)

        load(0)
        load(1)
        for dc in range(32):
            p = dc % 2
            wo_ld = sc.tok("d_wo%d" % p)
            xf_ld = sc.tok("d_xf%d" % p)

            def _t(t, dc=dc, p=p, wo_ld=wo_ld):
                sc.wait(t, "PE", wo_ld, self.att_tok, self.outv_tok, wvu_tok)
                if dc >= 2:
                    sc.wait(t, "PE", out_done[dc - 2])
                if dc >= 1:
                    sc.wait(t, "PE", ax_done[dc - 1])
                for th in range(2):
                    self.mmacc(t, p_wo[2 * p + th][:, :],
                               [(wob[p][:, c, :], big[:, c, 512 * th:512 * (th + 1)])
                                for c in range(32)])
                wo_cons[dc] = sc.tok("sp")
                for th in range(2):
                    self.mmacc(t, p_ax[th][:, :],
                               [(wvuT_sb[:, vc, 128 * dc:128 * (dc + 1)],
                                 self.outv_bf[:, vc, 512 * th:512 * (th + 1)])
                                for vc in range(2)])
            self.pe(_t)
            mm_tok = sc.tok("sp")

            def _v(v, dc=dc, mm_tok=mm_tok):
                sc.wait(v, "DVE", mm_tok, gb_tok)
                for th in range(2):
                    sl = slice(512 * th, 512 * (th + 1))
                    sc.inc(v.tensor_tensor(auxg[:, sl], p_ax[th][:, :], gate_b[:, sl],
                                           ALU.mult), "sv", 1)
                ax_done[dc] = sc.tok("sv")
            self.dve(_v)
            axv_tok = sc.tok("sv")

            def _p(g, dc=dc, p=p, axv_tok=axv_tok, xf_ld=xf_ld):
                sc.wait(g, "POOL", axv_tok, xf_ld, umu_tok)
                if dc >= 2:
                    sc.wait(g, "POOL", st_sent[dc - 2])
                sc.inc(g.tensor_tensor(outst[p][:, :], xfb[p][:, :], self.u1_b[:, :],
                                       ALU.mult), "sg", 1)
                sc.wait(g, "POOL", sc.tok("sg"))
                sc.inc(g.tensor_tensor(outst[p][:, :], outst[p][:, :], self.mu_b[:, :],
                                       ALU.subtract), "sg", 1)
                sc.wait(g, "POOL", sc.tok("sg"))
                sc.inc(g.tensor_tensor(outst[p][:, :], outst[p][:, :], auxg[:, :],
                                       ALU.add), "sg", 1)
                xf_cons[dc] = sc.tok("sg")
            self.pool(_p)
            pl_tok = sc.tok("sg")

            def _v2(v, dc=dc, p=p, pl_tok=pl_tok, mm_tok=mm_tok):
                sc.wait(v, "DVE", pl_tok, mm_tok)
                for th in range(2):
                    sl = slice(512 * th, 512 * (th + 1))
                    sc.inc(v.tensor_tensor(outst[p][:, sl], outst[p][:, sl],
                                           p_wo[2 * p + th][:, :], ALU.add), "sv", 1)
                out_done[dc] = sc.tok("sv")
            self.dve(_v2)
            st_tok = sc.tok("sv")

            def _s(sq, dc=dc, p=p, st_tok=st_tok):
                sc.wait(sq, "SYNC", st_tok)
                st_sent[dc] = sc.inc(
                    sq.dma_start(out=self.outT.ap()[128 * dc:128 * (dc + 1), :],
                                 in_=outst[p][:, :]), "d_o%d" % p, 16)
            self.syn(_s)
            if dc + 2 < 32:
                load(dc + 2)

    def finish(self):
        def _s(sq):
            sq.wait_ge(self.sem["d_o0"], self.cnt["d_o0"])
            sq.wait_ge(self.sem["d_o1"], self.cnt["d_o1"])
        self.syn(_s)
        if self.cnt["d_dbg"]:
            def _g(gq):
                gq.wait_ge(self.sem["d_dbg"], self.cnt["d_dbg"])
            self.pool(_g)



def _host_core_inputs(core, inputs):
    b = core % 4
    e = core // 4
    own_tiles = [2 * j + e for j in range(NOWN)]

    x_full = np.asarray(inputs["L0_attn_x"])
    x = x_full[b]
    own_rows = np.concatenate([np.arange(128 * t, 128 * (t + 1)) for t in own_tiles])
    xm = x[own_rows]

    wq = np.asarray(inputs["L0_attn_wq"])
    wo = np.asarray(inputs["L0_attn_wo"])
    wk = np.asarray(inputs["L0_attn_wk"])
    wv = np.asarray(inputs["L0_attn_wv"])
    Wq = np.asarray(inputs["L0_ffn_Wq"])
    Wr = np.asarray(inputs["L0_ffn_Wr"])
    ak = np.asarray(inputs["L0_ffn_aux_keys"])
    av = np.asarray(inputs["L0_ffn_aux_values"])
    Wvd = np.asarray(inputs["L0_ffn_Wvd"])
    Wvu = np.asarray(inputs["L0_ffn_Wvu"])
    Wu1 = np.asarray(inputs["L0_ffn_Wu1"])
    bu1 = np.asarray(inputs["L0_ffn_bu1"])
    Wu2 = np.asarray(inputs["L0_ffn_Wu2"])
    bu2 = np.asarray(inputs["L0_ffn_bu2"])
    gw = np.asarray(inputs["L0_ffn_gate_w1"])
    gb = np.asarray(inputs["L0_ffn_gate_bias"])
    lr = np.asarray(inputs["L0_ffn_log_rel"])

    d = {}
    batch_order = [(b + k) % 4 for k in range(4)]
    d["x_all_t"] = np.ascontiguousarray(
        x_full[batch_order].reshape(4, NT, 128, 32, 128).transpose(0, 1, 4, 3, 2))
    d["xm_t"] = np.ascontiguousarray(
        xm.reshape(NOWN, 128, 32, 128).transpose(3, 2, 0, 1).reshape(128, 32, NTOK))
    d["xmT"] = np.ascontiguousarray(xm.T)
    d["wq_t"] = np.ascontiguousarray(wq.reshape(32, 128, 32, 128).transpose(2, 1, 0, 3))
    d["wo_t"] = np.ascontiguousarray(wo.reshape(32, 128, 32, 128).transpose(2, 1, 0, 3))
    d["wk_t"] = np.ascontiguousarray(wk.reshape(32, 128, 128).transpose(1, 0, 2))
    d["wv_t"] = np.ascontiguousarray(wv.reshape(32, 128, 128).transpose(1, 0, 2))
    d["wqf_t"] = np.ascontiguousarray(Wq.T.reshape(32, 128, BOT).transpose(1, 0, 2))
    d["wr_T"] = np.ascontiguousarray(Wr.T)
    d["akT"] = np.ascontiguousarray(ak.T)
    d["avT"] = np.ascontiguousarray(av.T)
    d["wvdT"] = np.ascontiguousarray(Wvd.T)
    d["wvuT_t"] = np.ascontiguousarray(Wvu.T.reshape(2, 128, D).transpose(1, 0, 2))
    d["wu1T_t"] = np.ascontiguousarray(Wu1.reshape(8, 128, 32, 128).transpose(0, 3, 2, 1))
    d["wu2c"] = np.ascontiguousarray(Wu2.reshape(8, 128).T)
    d["bu1c"] = np.ascontiguousarray(bu1.reshape(8, 128).T)
    d["logrel_r"] = np.ascontiguousarray(lr[None, :])
    d["scal"] = np.array([[float(gw), float(gb), float(np.reshape(bu2, (-1,))[0]), 0.0]],
                         dtype=np.float32)

    masks = np.zeros((128, NOWN, 2, 128), dtype=np.float32)
    kp = np.arange(128)
    qi = np.arange(128)
    for js in range(NOWN):
        tj = own_tiles[js]
        for hh in range(2):
            kt_pos = 256 * js + 128 * hh + kp[:, None]
            q_pos = 128 * tj + qi[None, :]
            masks[:, js, hh, :] = np.where(kt_pos <= q_pos, 0.0, -1.0e9)
    d["masks"] = masks.astype(BFNP)


    d["c_ones_row_b"] = np.ones((1, 128), dtype=BFNP)
    d["c_ones_col_b"] = np.ones((128, 1), dtype=BFNP)
    d["c_ones_row_f"] = np.ones((1, 128), dtype=np.float32)
    d["c_ones_mat_b"] = np.ones((128, 128), dtype=BFNP)
    d["c_eye_b"] = np.eye(128, dtype=np.float32).astype(BFNP)
    d["c_eye_f"] = np.eye(128, dtype=np.float32)
    d["c_sel16_f"] = np.eye(16, dtype=np.float32)
    selown = np.zeros((16, 8), dtype=np.float32)
    for j in range(NOWN):
        selown[own_tiles[j], j] = 1.0
    d["c_selown_f"] = selown
    for k in list(d):
        if d[k].dtype == np.float64:
            d[k] = d[k].astype(np.float32)
    return d


_CACHE = {}


def kernel(**inputs):
    if "nc" not in _CACHE:
        _CACHE["nc"] = Builder(debug=False).build()
    nc = _CACHE["nc"]
    in_maps = [_host_core_inputs(c, inputs) for c in range(8)]
    res = run_bass_kernel_spmd(nc, in_maps, core_ids=list(range(8)))
    out = np.zeros((B, T, D), dtype=np.float32)
    for c in range(8):
        b = c % 4
        e = c // 4
        own_tiles = [2 * j + e for j in range(NOWN)]
        o = np.asarray(res.results[c]["outT"]).T
        for j, t in enumerate(own_tiles):
            out[b, 128 * t:128 * (t + 1), :] = o[128 * j:128 * (j + 1), :]
    return out
